# version 19
# baseline (speedup 1.0000x reference)
import math
from collections import defaultdict
from contextlib import ExitStack

import numpy as np
import ml_dtypes

import concourse.bass as bass
import concourse.mybir as mybir
from concourse.bass_utils import run_bass_kernel_spmd
from concourse.aot_env import allow_ici_all_to_all

F32 = mybir.dt.float32
BF16 = mybir.dt.bfloat16
AF = mybir.ActivationFunctionType
AX = mybir.AxisListType
OP = mybir.AluOpType

NCORES = 8
S = 8192
D = 2048
HD = 128
TPC = 1024
NT = 8
DC = 16
FF = 8192
NFC = 64
NPC = 16
PCF = 4
EPS = 1e-5
ISQ = 1.0 / math.sqrt(HD)
NEG = -1.0e9

ENGS = ("sync", "tensor", "scalar", "vector", "gpsimd")
OWN = {"sP": "tensor"}


class Prog:

    def __init__(self):
        self.steps = {e: [] for e in ENGS}
        self.cnt = defaultdict(int)
        self.waited = {e: defaultdict(int) for e in ENGS}
        self.last_ev = {}

    def add(self, eng, emit, deps=(), inc=None, n=1, chain=True):
        deps = list(deps)
        if chain and eng in ("vector", "scalar"):
            deps.append(self.last_ev.get(eng))
        ev = None
        if inc is not None:
            self.cnt[inc] += n
            ev = (inc, self.cnt[inc])
            if chain and eng in ("vector", "scalar"):
                self.last_ev[eng] = ev
        deps = tuple(d for d in deps if d is not None)
        self.steps[eng].append((emit, deps, inc, n))
        return ev

    def flush(self, block, sems):
        for name in ENGS:
            if not self.steps[name]:
                continue
            steps = self.steps[name]
            self.steps[name] = []
            waited = self.waited[name]

            def body(eng, steps=steps, waited=waited, name=name):
                for emit, deps, inc, n in steps:
                    for (s_, v) in deps:
                        if OWN.get(s_) == name:
                            continue
                        if waited[s_] < v:
                            eng.wait_ge(sems[s_], v)
                            waited[s_] = v
                    inst = emit(eng)
                    if inc is not None:
                        inst.then_inc(sems[inc], n)
            getattr(block, name)(body)


def build_nc():
    nc = bass.Bass("TRN2", target_bir_lowering=False, debug=False)

    xs = nc.dram_tensor("xs", [TPC, D], F32, kind="ExternalInput")
    wqkv = nc.dram_tensor("wqkv", [128, 96 * 128], F32, kind="ExternalInput")
    wo_sh = nc.dram_tensor("wo_sh", [256, D], F32, kind="ExternalInput")
    w0_sh = nc.dram_tensor("w0_sh", [1024, D], F32, kind="ExternalInput")
    w1_sh = nc.dram_tensor("w1_sh", [1024, D], F32, kind="ExternalInput")
    b0t = nc.dram_tensor("b0t", [128, NFC], F32, kind="ExternalInput")
    b1bc = nc.dram_tensor("b1bc", [128, D], F32, kind="ExternalInput")
    tri = nc.dram_tensor("tri", [128, 128], F32, kind="ExternalInput")
    identd = nc.dram_tensor("identd", [128, 128], BF16, kind="ExternalInput")
    y = nc.dram_tensor("y", [TPC, D], F32, kind="ExternalOutput")

    xn_in = [nc.dram_tensor("xn_in%d" % i, [D, TPC // 2], BF16, kind="Internal")
             for i in range(2)]
    xn_out = [nc.dram_tensor("xn_out%d" % i, [NCORES * D, TPC // 2], BF16,
                             kind="Internal", addr_space="Shared")
              for i in range(2)]
    wo_in = nc.dram_tensor("wo_in", [256, D], BF16, kind="Internal")
    wo_out = nc.dram_tensor("wo_out", [D, D], BF16, kind="Internal",
                            addr_space="Shared")
    w0_in = nc.dram_tensor("w0_in", [1024, D], BF16, kind="Internal")
    w0_out = nc.dram_tensor("w0_out", [FF, D], BF16, kind="Internal",
                            addr_space="Shared")
    w1_in = nc.dram_tensor("w1_in", [1024, D], BF16, kind="Internal")
    w1_out = nc.dram_tensor("w1_out", [FF, D], BF16, kind="Internal",
                            addr_space="Shared")
    ao_in = [nc.dram_tensor("ao_in%d" % h, [S, 128], BF16, kind="Internal")
             for h in range(2)]
    ao_out = [nc.dram_tensor("ao_out%d" % h, [S, 128], BF16, kind="Internal")
              for h in range(2)]
    x1_sp = nc.dram_tensor("x1_sp", [TPC, D], F32, kind="Internal")

    rg = [list(range(NCORES))]
    prog = Prog()

    with ExitStack() as st:
        b0t_sb = st.enter_context(nc.sbuf_tensor("b0t_sb", [128, NFC], F32))
        tri_sb = st.enter_context(nc.sbuf_tensor("tri_sb", [128, 128], F32))
        id_sb = st.enter_context(nc.sbuf_tensor("id_sb", [128, 128], BF16))
        eps_sb = st.enter_context(nc.sbuf_tensor("eps_sb", [128, 1], F32))
        stats = st.enter_context(nc.sbuf_tensor("stats", [128, 16], F32))
        sqs = st.enter_context(nc.sbuf_tensor("sqs", [128, D], F32))
        ps6 = st.enter_context(nc.psum_tensor("ps6", [128, 3072], F32))

        sems = {}
        for s_ in ("sP", "sA", "sV",
                   "sC", "sX", "sXq1", "sXq2", "sXN", "sXN1", "sAO0", "sAO1",
                   "sWB", "sWO", "sXG",
                   "sRHS00", "sRHS01", "sRHS10", "sRHS11",
                   "sRHG00", "sRHG01", "sRHG10", "sRHG11", "sW0a", "sW0b", "sW0c", "sW0d", "sW1a", "sW1b",
                   "sXS0", "sXS1", "sAOT0", "sAOT1", "sX1W0", "sX1W1",
                   "sX1R0", "sX1R1", "sY0", "sY1",
                   "ccX", "ccWo", "ccW0", "ccW1", "ccAO"):
            sems[s_] = st.enter_context(nc.semaphore(s_))
        PE, SC, VE = "sP", "sA", "sV"

        block = st.enter_context(nc.Block())

        def dma(eng_name, sem, out, in_, deps=(), chain=True):
            return prog.add(eng_name,
                            lambda e, o=out, i=in_: e.dma_start(out=o, in_=i),
                            deps=deps, inc=sem, n=16, chain=chain)

        ln_par = [0]

        def ln(tile_ap, out_ap, ev_tile, extra_apply=()):
            p8 = (ln_par[0] % 2) * 8
            ln_par[0] += 1
            ssum = stats[:, p8 + 0:p8 + 1]
            negm = stats[:, p8 + 1:p8 + 2]
            m = stats[:, p8 + 2:p8 + 3]
            msq = stats[:, p8 + 3:p8 + 4]
            var = stats[:, p8 + 4:p8 + 5]
            u = stats[:, p8 + 5:p8 + 6]
            rstd = stats[:, p8 + 6:p8 + 7]
            sumsq = stats[:, p8 + 7:p8 + 8]

            ev_r = prog.add("vector",
                            lambda e: e.reduce_sum(ssum, tile_ap, AX.X),
                            deps=[ev_tile], inc=VE)
            ev_q = prog.add("scalar",
                            lambda e: e.activation(sqs[:, :], tile_ap, AF.Square,
                                                   accum_out=sumsq),
                            deps=[ev_tile], inc=SC)

            def _m(e):
                e.tensor_scalar_mul(m, ssum, 1.0 / D)
                return e.tensor_scalar_mul(negm, ssum, -1.0 / D)
            ev_m = prog.add("vector", _m, deps=[ev_r], inc=VE)
            ev_mq = prog.add("vector",
                             lambda e: e.tensor_tensor(msq, m, m, OP.mult),
                             deps=[ev_m], inc=VE)
            ev_v = prog.add("vector",
                            lambda e: e.scalar_tensor_tensor(
                                var, sumsq, 1.0 / D, msq, OP.mult, OP.subtract),
                            deps=[ev_q, ev_mq], inc=VE)

            ev_u = prog.add("scalar",
                            lambda e: e.activation(u, var, AF.Ln,
                                                   bias=eps_sb[:, :]),
                            deps=[ev_v], inc=SC)
            ev_rs = prog.add("scalar",
                             lambda e: e.activation(rstd, u, AF.Exp, scale=-0.5),
                             deps=[ev_u], inc=SC)

            return prog.add("vector",
                            lambda e: e.tensor_scalar(out_ap, tile_ap, negm, rstd,
                                                      OP.add, OP.mult),
                            deps=[ev_rs, *extra_apply], inc=VE)

        stW = st.enter_context(ExitStack())
        wq_sb = stW.enter_context(nc.sbuf_tensor("wq_sb", [128, 96 * 128], BF16))

        ev_c_b0 = dma("sync", "sC", b0t_sb[:, :], b0t[:, :])
        ev_c_tri = dma("sync", "sC", tri_sb[:, :], tri[:, :])
        ev_c_id = dma("sync", "sC", id_sb[:, :], identd[:, :])
        ev_c_all = ev_c_id
        prog.add("vector", lambda e: e.memset(eps_sb[:, :], EPS), inc=VE)


        prog.flush(block, sems)

        with ExitStack() as st1:
            xall = st1.enter_context(nc.sbuf_tensor("xall", [128, NT * D], F32))
            xn_t = st1.enter_context(nc.sbuf_tensor("xn_t", [128, 2 * D], BF16))
            xnT = st1.enter_context(nc.sbuf_tensor("xnT", [128, DC * TPC], BF16))
            ps_t = st1.enter_context(nc.psum_tensor("ps_t", [128, 2048], BF16))
            lnst = st1.enter_context(nc.sbuf_tensor("lnst", [128, 64], F32))

            qeng = ["sync", "scalar", "gpsimd", "gpsimd"]
            qsem = ["sX", "sXq1", "sXq2", "sXG"]
            ev_xq = []
            for q in range(4):
                srcap = xs.ap()[q * 256:(q + 1) * 256, :]
                srcap = srcap.rearrange("(tt p) d -> p tt d", p=128)
                dstap = xall[:, q * 2 * D:(q + 1) * 2 * D]
                dstap = dstap.rearrange("p (tt d) -> p tt d", d=D)
                ev_xq.append(dma(qeng[q], qsem[q], dstap, srcap, chain=False))
            ev_wq = dma("gpsimd", "sWB", wq_sb[:, :], wqkv[:, :])

            sum8 = lnst[:, 0:8]
            sq8 = lnst[:, 8:16]
            m8 = lnst[:, 16:24]
            negm8 = lnst[:, 24:32]
            msq8 = lnst[:, 32:40]
            var8 = lnst[:, 40:48]
            u8 = lnst[:, 48:56]
            rstd8 = lnst[:, 56:64]

            tp_pe = {}
            tp_ev = {}
            ev_agx = {}
            for hh in range(2):
                tl = slice(4 * hh, 4 * hh + 4)
                hdeps = [ev_xq[2 * hh], ev_xq[2 * hh + 1]]
                ev_red = prog.add(
                    "vector",
                    lambda e, tl=tl, hh=hh: e.reduce_sum(
                        sum8[:, tl],
                        xall[:, hh * 4 * D:(hh + 1) * 4 * D].rearrange(
                            "p (t d) -> p t d", d=D),
                        AX.X),
                    deps=hdeps, inc=VE)
                ev_sq = None
                for tt in range(4 * hh, 4 * hh + 4):
                    ev_sq = prog.add(
                        "scalar",
                        lambda e, tt=tt: e.activation(
                            sqs[:, :], xall[:, tt * D:(tt + 1) * D], AF.Square,
                            accum_out=sq8[:, tt:tt + 1]),
                        deps=[ev_xq[tt // 2]], inc=SC)
                prog.add("vector",
                         lambda e, tl=tl: e.tensor_scalar_mul(
                             m8[:, tl], sum8[:, tl], 1.0 / D),
                         deps=[ev_red], inc=VE)
                prog.add("vector",
                         lambda e, tl=tl: e.tensor_scalar_mul(
                             negm8[:, tl], sum8[:, tl], -1.0 / D),
                         inc=VE)
                prog.add("vector",
                         lambda e, tl=tl: e.tensor_tensor(
                             msq8[:, tl], m8[:, tl], m8[:, tl], OP.mult),
                         inc=VE)
                ev_v8 = prog.add(
                    "vector",
                    lambda e, tl=tl: e.scalar_tensor_tensor(
                        var8[:, tl], sq8[:, tl], 1.0 / D, msq8[:, tl],
                        OP.mult, OP.subtract),
                    deps=[ev_sq], inc=VE)
                prog.add("scalar",
                         lambda e, tl=tl: e.activation(
                             u8[:, tl], var8[:, tl], AF.Ln, bias=eps_sb[:, :]),
                         deps=[ev_v8], inc=SC)
                ev_r8h = prog.add(
                    "scalar",
                    lambda e, tl=tl: e.activation(
                        rstd8[:, tl], u8[:, tl], AF.Exp, scale=-0.5),
                    inc=SC)

                for tt in range(4 * hh, 4 * hh + 4):
                    xtile = xall[:, tt * D:(tt + 1) * D]
                    xnt = xn_t[:, (tt % 2) * D:(tt % 2 + 1) * D]
                    ev_ap = prog.add(
                        "vector",
                        lambda e, xtile=xtile, xnt=xnt, tt=tt: e.tensor_scalar(
                            xnt, xtile, negm8[:, tt:tt + 1], rstd8[:, tt:tt + 1],
                            OP.add, OP.mult),
                        deps=[ev_r8h, tp_pe.get((tt - 2, 1))], inc=VE)
                    for half in range(2):
                        def _tp(e, xnt=xnt, half=half):
                            for k in range(8):
                                dcc = half * 8 + k
                                i = e.transpose(
                                    ps_t[:, (half * 8 + k) * 128:(half * 8 + k + 1) * 128],
                                    xnt[:, dcc * 128:(dcc + 1) * 128], id_sb[:, :])
                            return i
                        pv = (tt, 0) if half else (tt - 1, 1)
                        tp_pe[(tt, half)] = prog.add(
                            "tensor", _tp,
                            deps=[ev_ap, ev_c_all, tp_ev.get(pv)], inc=PE)

                        def _te(e, tt=tt, half=half):
                            for k in range(8):
                                dcc = half * 8 + k
                                i = e.tensor_copy(
                                    xnT[:, dcc * TPC + tt * 128:dcc * TPC + (tt + 1) * 128],
                                    ps_t[:, (half * 8 + k) * 128:(half * 8 + k + 1) * 128])
                            return i
                        tp_ev[(tt, half)] = prog.add("vector", _te,
                                                     deps=[tp_pe[(tt, half)]], inc=VE)

                th = hh
                gate = tp_ev[(4 * th + 3, 1)]
                ev_wr = None
                for dcc in range(DC):
                    ev_wr = dma("sync", "sXN" if th == 0 else "sXN1",
                                xn_in[th][dcc * 128:(dcc + 1) * 128, :],
                                xnT[:, dcc * TPC + th * 512:
                                    dcc * TPC + th * 512 + 512],
                                deps=[gate])
                ev_agx[th] = prog.add(
                    "gpsimd",
                    lambda e, th=th: e.collective_compute(
                        "AllGather", OP.bypass, replica_groups=rg,
                        ins=[xn_in[th].ap().opt()], outs=[xn_out[th].ap().opt()]),
                    deps=[ev_wr, ev_agx.get(0) if th else None], inc="ccX",
                    n=1)

            prog.flush(block, sems)

        with ExitStack() as stB:
            qsb = stB.enter_context(nc.sbuf_tensor("qsb", [128, 2 * S], BF16))
            ksb = stB.enter_context(nc.sbuf_tensor("ksb", [128, 2 * S], BF16))
            vsb = stB.enter_context(nc.sbuf_tensor("vsb", [128, 2 * S], BF16))

            with ExitStack() as st2:
                rhs_sb = st2.enter_context(
                    nc.sbuf_tensor("rhs_sb", [128, 2 * DC * 1024], BF16))
                dest = {0: qsb, 1: ksb, 2: vsb}

                ev_rhs = {}
                pj_pe = {}
                pj_ev = {}
                slot_prev = {}
                ci = 0
                for hpass in range(2):
                    for pr in range(8):
                        slot = pr % 2
                        evs = []
                        th = hpass
                        for qh in range(2):
                            srcap = xn_out[th].ap()[2048 * pr + qh * 1024:
                                                    2048 * pr + (qh + 1) * 1024, :]
                            srcap = srcap.rearrange("(dc p) t -> p dc t", p=128)
                            dst = rhs_sb[:, (slot * DC + qh * 8) * 1024:
                                         (slot * DC + (qh + 1) * 8) * 1024]
                            dst = dst.rearrange("p (dc t2) -> p dc t2", t2=1024)
                            dst = dst[:, :, th * 512:(th + 1) * 512]
                            eng = "sync" if qh == 0 else "gpsimd"
                            sem = "sRH%s%d%d" % ("SG"[qh], slot, th)
                            evs.append(dma(eng, sem, dst, srcap,
                                           deps=[ev_agx[th],
                                                 slot_prev.get(slot)]))
                        ev_rhs[(hpass, pr)] = evs

                        for b in range(6):
                            tc = 2 * pr + hpass
                            psl = ps6[:, (ci % 6) * 512:(ci % 6 + 1) * 512]

                            def _mm(e, b=b, half=hpass, psl=psl, slot=slot):
                                for dcc in range(DC):
                                    i = e.matmul(
                                        psl,
                                        wq_sb[:, (b * DC + dcc) * 128:
                                              (b * DC + dcc + 1) * 128],
                                        rhs_sb[:, (slot * DC + dcc) * 1024
                                               + half * 512:
                                               (slot * DC + dcc) * 1024
                                               + half * 512 + 512],
                                        start=(dcc == 0), stop=(dcc == DC - 1))
                                return i
                            pj_pe[ci] = prog.add(
                                "tensor", _mm,
                                deps=evs + [ev_wq, pj_ev.get(ci - 6)],
                                inc=PE)

                            h_l, ty = b // 3, b % 3
                            dsb = dest[ty]
                            pj_ev[ci] = prog.add(
                                "vector",
                                lambda e, dsb=dsb, h_l=h_l, tc=tc, psl=psl:
                                    e.tensor_copy(
                                        dsb[:, h_l * S + tc * 512:
                                            h_l * S + (tc + 1) * 512],
                                        psl),
                                deps=[pj_pe[ci]], inc=VE)
                            ci += 1
                        slot_prev[slot] = pj_pe[ci - 1]

                ev_proj_done = pj_ev[ci - 1]

                ev_bn = dma("gpsimd", "sWB", wo_in[:, :], wo_sh[:, :],
                            deps=[ev_agx[1]])
                ev_bn = dma("gpsimd", "sWB", w0_in[:, :], w0_sh[:, :])
                ev_bn = dma("gpsimd", "sWB", w1_in[:, :], w1_sh[:, :])
                ev_agwo = prog.add("gpsimd",
                                   lambda e: e.collective_compute(
                                       "AllGather", OP.bypass, replica_groups=rg,
                                       ins=[wo_in.ap().opt()], outs=[wo_out.ap().opt()]),
                                   deps=[ev_bn, ev_rhs[(1, 7)][0], ev_rhs[(1, 7)][1]],
                                   inc="ccWo")
                ev_agw0 = prog.add("gpsimd",
                                   lambda e: e.collective_compute(
                                       "AllGather", OP.bypass, replica_groups=rg,
                                       ins=[w0_in.ap().opt()], outs=[w0_out.ap().opt()]),
                                   deps=[ev_agwo], inc="ccW0")
                ev_agw1 = prog.add("gpsimd",
                                   lambda e: e.collective_compute(
                                       "AllGather", OP.bypass, replica_groups=rg,
                                       ins=[w1_in.ap().opt()], outs=[w1_out.ap().opt()]),
                                   deps=[ev_agw0], inc="ccW1")

                prog.flush(block, sems)

            stV = stB.enter_context(ExitStack())
            vsb2 = stV.enter_context(
                nc.sbuf_tensor("vsb2", [128, 2 * 64 * 129], BF16))
            with ExitStack() as st3:
                ps_t2 = st3.enter_context(
                    nc.psum_tensor("ps_t2", [128, 2048], BF16))
                ev_ones = prog.add(
                    "vector",
                    lambda e: e.memset(
                        vsb2[:, :].rearrange(
                            "p (c w) -> p c w", w=129)[:, :, 128:129],
                        1.0),
                    deps=[ev_proj_done], inc=VE)
                vt_ev = {}
                bidx = 0
                for h_l in range(2):
                    for bb in range(8):
                        half = bidx % 2

                        def _vt(e, h_l=h_l, bb=bb, half=half):
                            for k in range(8):
                                kcc = bb * 8 + k
                                i = e.transpose(
                                    ps_t2[:, (half * 8 + k) * 128:
                                          (half * 8 + k + 1) * 128],
                                    vsb[:, h_l * S + kcc * 128:
                                        h_l * S + (kcc + 1) * 128],
                                    id_sb[:, :])
                            return i
                        ev_vt = prog.add("tensor", _vt,
                                         deps=[ev_proj_done,
                                               vt_ev.get(bidx - 2)], inc=PE)

                        def _ve(e, h_l=h_l, bb=bb, half=half):
                            for k in range(8):
                                kcc = bb * 8 + k
                                i = e.tensor_copy(
                                    vsb2[:, (h_l * 64 + kcc) * 129:
                                         (h_l * 64 + kcc) * 129 + 128],
                                    ps_t2[:, (half * 8 + k) * 128:
                                          (half * 8 + k + 1) * 128])
                            return i
                        vt_ev[bidx] = prog.add("vector", _ve,
                                               deps=[ev_vt, ev_ones], inc=VE)
                        bidx += 1
                ev_vt_done = vt_ev[bidx - 1]
                prog.flush(block, sems)

            with ExitStack() as st4:
                ps_o = st4.enter_context(nc.psum_tensor("ps_o", [128, 512], F32))
                pt_sb = st4.enter_context(
                    nc.sbuf_tensor("pt_sb", [128, 3 * 1024], BF16))
                aoev = st4.enter_context(
                    nc.sbuf_tensor("aoev", [128, 2 * 1024], BF16))
                rls = st4.enter_context(nc.sbuf_tensor("rls", [128, 16], F32))

                S_ps = [ps6[:, 0:1024], ps6[:, 1024:2048]]

                def otile(t4):
                    if t4 < 6:
                        base = 2048 + (t4 // 3) * 512 + (t4 % 3) * 129
                        return ps6[:, base:base + 129]
                    return ps_o[:, (t4 - 6) * 129:(t4 - 6) * 129 + 129]

                gstep = 0
                hg = 0
                ev_exp = {}
                ev_av = {}
                ev_onrm = {}
                ev_aowr = {}

                def plan_kc(h_l, g, kc):
                    diag = kc >= 8 * g
                    c0 = 128 * (kc - 8 * g) if diag else 0
                    pieces = ([(c0, 512), (512, 1024)] if c0 < 512
                              else [(c0, 1024)])
                    return diag, c0, tuple(pieces)

                def emit_s_exp(h_l, g, kc, gs):
                    diag, c0, pieces = plan_kc(h_l, g, kc)
                    spar = gs % 2
                    Sp = S_ps[spar]
                    pt = pt_sb[:, (gs % 3) * 1024:(gs % 3 + 1) * 1024]

                    def _smm(e, h_l=h_l, g=g, kc=kc, pieces=pieces, Sp=Sp):
                        for (a, b_) in pieces:
                            i = e.matmul(
                                Sp[:, a:b_],
                                ksb[:, h_l * S + kc * 128:h_l * S + (kc + 1) * 128],
                                qsb[:, h_l * S + g * 1024 + a:
                                    h_l * S + g * 1024 + b_],
                                start=True, stop=True)
                        return i
                    ev_s = prog.add("tensor", _smm,
                                    deps=[ev_vt_done, ev_exp.get(gs - 2)], inc=PE)
                    dep_exp = ev_s
                    if diag:
                        dep_exp = prog.add(
                            "vector",
                            lambda e, Sp=Sp, c0=c0:
                                e.tensor_tensor(Sp[:, c0:c0 + 128],
                                                Sp[:, c0:c0 + 128],
                                                tri_sb[:, :], OP.add),
                            deps=[ev_s, ev_c_all], inc=VE)
                    ev_exp[gs] = prog.add(
                        "scalar",
                        lambda e, Sp=Sp, pt=pt, c0=c0:
                            e.activation(pt[:, c0:1024], Sp[:, c0:1024],
                                         AF.Exp, scale=ISQ),
                        deps=[dep_exp, ev_av.get(gs - 3)], inc=SC)

                def emit_av(h_l, g, kc, gs, hg):
                    diag, c0, pieces = plan_kc(h_l, g, kc)
                    t4min = c0 // 128
                    pt = pt_sb[:, (gs % 3) * 1024:(gs % 3 + 1) * 1024]

                    def _av(e, h_l=h_l, kc=kc, t4min=t4min, pt=pt,
                            first=(kc == 0)):
                        for t4 in range(t4min, 8):
                            i = e.matmul(
                                otile(t4),
                                pt[:, t4 * 128:(t4 + 1) * 128],
                                vsb2[:, (h_l * 64 + kc) * 129:
                                     (h_l * 64 + kc) * 129 + 129],
                                start=(first and t4 in (0, 3, 6)), stop=False,
                                skip_group_check=True)
                        return i
                    deps_a = [ev_exp[gs]]
                    if kc == 0:
                        deps_a.append(ev_onrm.get(hg - 1))
                    ev_av[gs] = prog.add("tensor", _av, deps=deps_a, inc=PE)

                ev_a2a_h = {}
                for h_l in range(2):
                    for g in range(8):
                        nkc = 8 * (g + 1)
                        for kc in range(nkc + 1):
                            if kc < nkc:
                                emit_s_exp(h_l, g, kc, gstep + kc)
                            if kc >= 1:
                                emit_av(h_l, g, kc - 1, gstep + kc - 1, hg)
                        gstep += nkc

                        def _rc(e, hg=hg):
                            for t4 in range(8):
                                i = e.reciprocal(
                                    rls[:, (hg % 2) * 8 + t4:
                                        (hg % 2) * 8 + t4 + 1],
                                    otile(t4)[:, 128:129])
                            return i
                        ev_rc = prog.add("vector", _rc,
                                         deps=[ev_av[gstep - 1]], inc=VE)

                        def _nm(e, hg=hg):
                            for t4 in range(8):
                                i = e.tensor_scalar_mul(
                                    aoev[:, (hg % 2) * 1024 + t4 * 128:
                                         (hg % 2) * 1024 + (t4 + 1) * 128],
                                    otile(t4)[:, 0:128],
                                    rls[:, (hg % 2) * 8 + t4:
                                        (hg % 2) * 8 + t4 + 1])
                            return i
                        ev_onrm[hg] = prog.add(
                            "vector", _nm,
                            deps=[ev_rc, ev_aowr.get(hg - 2)], inc=VE)

                        src = aoev[:, (hg % 2) * 1024:(hg % 2 + 1) * 1024]
                        src = src.rearrange("q (t4 d) -> q t4 d", d=128)
                        dst = ao_in[h_l].ap()[TPC * g:TPC * (g + 1), :]
                        dst = dst.rearrange("(t4 q) d -> q t4 d", q=128)
                        ev_aowr[hg] = dma("sync", "sAO%d" % (hg % 2), dst, src,
                                          deps=[ev_onrm[hg]])
                        hg += 1

                    def _a2a(e, h_l=h_l):
                        with allow_ici_all_to_all():
                            return e.collective_compute(
                                "AllToAll", OP.bypass, replica_groups=rg,
                                ins=[ao_in[h_l].ap().opt()],
                                outs=[ao_out[h_l].ap().opt()])
                    ev_a2a_h[h_l] = prog.add(
                        "gpsimd", _a2a,
                        deps=[ev_aowr[hg - 1], ev_aowr[hg - 2],
                              ev_a2a_h.get(0) if h_l else None],
                        inc="ccAO")

                ev_att_done = ev_av[gstep - 1]
                ev_onrm_last = ev_onrm[hg - 1]
                prog.flush(block, sems)

        stW.close()

        with ExitStack() as stC:
            xn2T = stC.enter_context(nc.sbuf_tensor("xn2T", [128, DC * TPC], BF16))

            with ExitStack() as st5:
                wo_sb = st5.enter_context(nc.sbuf_tensor("wo_sb", [128, DC * D], BF16))
                aot = st5.enter_context(
                    nc.sbuf_tensor("aot", [128, 2 * DC * 128], BF16))
                aor = st5.enter_context(
                    nc.sbuf_tensor("aor", [128, 2 * DC * 128], BF16))
                prjb = st5.enter_context(nc.sbuf_tensor("prjb", [128, NT * D], BF16))
                x1t = st5.enter_context(nc.sbuf_tensor("x1t", [128, 2 * D], F32))
                xst = st5.enter_context(nc.sbuf_tensor("xst", [128, 2 * D], F32))
                xn2t = st5.enter_context(nc.sbuf_tensor("xn2t", [128, 2 * D], BF16))
                lnst3 = st5.enter_context(nc.sbuf_tensor("lnst3", [128, 64], F32))
                lnbn = st5.enter_context(nc.sbuf_tensor("lnbn", [128, 48], F32))
                ps_t3 = st5.enter_context(nc.psum_tensor("ps_t3", [128, 2048], BF16))

                ev_wo = None
                for dcc in range(DC):
                    ev_wo = dma("scalar", "sWO", wo_sb[:, dcc * D:(dcc + 1) * D],
                                wo_out[dcc * 128:(dcc + 1) * 128, :],
                                deps=[ev_agwo, ev_att_done], chain=False)

                sum8b = lnst3[:, 0:8]
                sq8b = lnst3[:, 8:16]
                m8b = lnst3[:, 16:24]
                negm8b = lnst3[:, 24:32]
                msq8b = lnst3[:, 32:40]
                var8b = lnst3[:, 40:48]
                u8b = lnst3[:, 48:56]
                rstd8b = lnst3[:, 56:64]

                ev_pj = {}
                ev_prjh_ev = {}
                ev_redb = {}
                ev_sqb = {}
                atp_pe = {}
                atp_ev = {}
                ev_r8b_h = {}
                ev_nr8b_h = {}
                ev_x1wr = {}
                ev_xn2 = {}
                tp3_pe = {}
                tp3_ev = {}
                ev_ln2_t = {}
                pst3_prev = {}

                def ln1_smalls(hh):
                    tl = slice(hh * 4, hh * 4 + 4)
                    prog.add("vector",
                             lambda e, tl=tl: e.tensor_scalar_mul(
                                 m8b[:, tl], sum8b[:, tl], 1.0 / D),
                             deps=[ev_redb[4 * hh + 3]], inc=VE)
                    prog.add("vector",
                             lambda e, tl=tl: e.tensor_scalar_mul(
                                 negm8b[:, tl], sum8b[:, tl], -1.0 / D),
                             inc=VE)
                    prog.add("vector",
                             lambda e, tl=tl: e.tensor_tensor(
                                 msq8b[:, tl], m8b[:, tl], m8b[:, tl],
                                 OP.mult),
                             inc=VE)
                    ev_v = prog.add(
                        "vector",
                        lambda e, tl=tl: e.scalar_tensor_tensor(
                            var8b[:, tl], sq8b[:, tl], 1.0 / D, msq8b[:, tl],
                            OP.mult, OP.subtract),
                        deps=[ev_sqb[4 * hh + 3]], inc=VE)
                    prog.add("scalar",
                             lambda e, tl=tl: e.activation(
                                 u8b[:, tl], var8b[:, tl], AF.Ln,
                                 bias=eps_sb[:, :]),
                             deps=[ev_v], inc=SC)
                    ev_r8b_h[hh] = prog.add(
                        "scalar",
                        lambda e, tl=tl: e.activation(
                            rstd8b[:, tl], u8b[:, tl], AF.Exp, scale=-0.5),
                        inc=SC)
                    ev_nr8b_h[hh] = prog.add(
                        "vector",
                        lambda e, tl=tl: e.tensor_tensor(
                            negm8b[:, tl], negm8b[:, tl], rstd8b[:, tl],
                            OP.mult),
                        deps=[ev_r8b_h[hh]], inc=VE)

                def stage3_chain(tt):
                    par = tt % 2
                    ev_xs = dma("sync", "sXS%d" % (tt % 2),
                                xst[:, par * D:(par + 1) * D],
                                xs[tt * 128:(tt + 1) * 128, :],
                                deps=[ev_xn2.get(tt - 2)])
                    ev_ap1 = prog.add(
                        "scalar",
                        lambda e, tt=tt, par=par: e.activation(
                            x1t[:, par * D:(par + 1) * D],
                            prjb[:, tt * D:(tt + 1) * D],
                            AF.Identity,
                            bias=negm8b[:, tt:tt + 1],
                            scale=rstd8b[:, tt:tt + 1]),
                        deps=[ev_nr8b_h[tt // 4], ev_x1wr.get(tt - 2)], inc=SC)
                    ev_x1 = prog.add(
                        "vector",
                        lambda e, par=par: e.tensor_tensor(
                            x1t[:, par * D:(par + 1) * D],
                            x1t[:, par * D:(par + 1) * D],
                            xst[:, par * D:(par + 1) * D], OP.add),
                        deps=[ev_ap1, ev_xs], inc=VE)
                    ev_x1wr[tt] = dma("sync", "sX1W%d" % (tt % 2),
                                      x1_sp[tt * 128:(tt + 1) * 128, :],
                                      x1t[:, par * D:(par + 1) * D],
                                      deps=[ev_x1])

                    sl8 = stats[:, par * 8:par * 8 + 8]
                    mv2 = sl8[:, 0:2]
                    u2s = sl8[:, 2:3]
                    rstd2s = sl8[:, 3:4]
                    nr2s = sl8[:, 4:5]

                    def _bn2(e, par=par):
                        for c in range(4):
                            i = e.bn_stats(
                                lnbn[:, par * 24 + c * 6:par * 24 + (c + 1) * 6],
                                x1t[:, par * D + c * 512:par * D + (c + 1) * 512])
                        return i
                    prog.add("vector", _bn2, deps=[ev_x1], inc=VE)
                    ev_bn2 = prog.add(
                        "vector",
                        lambda e, mv2=mv2, par=par: e.bn_aggr(
                            mv2, lnbn[:, par * 24:par * 24 + 24]),
                        inc=VE)
                    prog.add("scalar",
                             lambda e, u2s=u2s, mv2=mv2: e.activation(
                                 u2s, mv2[:, 1:2], AF.Ln, bias=eps_sb[:, :]),
                             deps=[ev_bn2], inc=SC)
                    ev_rs2 = prog.add(
                        "scalar",
                        lambda e, rstd2s=rstd2s, u2s=u2s: e.activation(
                            rstd2s, u2s, AF.Exp, scale=-0.5),
                        inc=SC)
                    ev_nr2 = prog.add(
                        "vector",
                        lambda e, nr2s=nr2s, mv2=mv2, rstd2s=rstd2s:
                            e.scalar_tensor_tensor(
                                nr2s, mv2[:, 0:1], -1.0, rstd2s,
                                OP.mult, OP.mult),
                        deps=[ev_rs2], inc=VE)
                    ev_ln2 = prog.add(
                        "scalar",
                        lambda e, par=par, nr2s=nr2s, rstd2s=rstd2s:
                            e.activation(
                                xn2t[:, par * D:(par + 1) * D],
                                x1t[:, par * D:(par + 1) * D],
                                AF.Identity, bias=nr2s, scale=rstd2s),
                        deps=[ev_nr2, tp3_pe.get((tt - 2, 1))], inc=SC)
                    ev_ln2_t[tt] = ev_ln2

                def stage3_tp(tt):
                    par = tt % 2
                    ev_ln2 = ev_ln2_t[tt]
                    for half in range(2):
                        def _tp3(e, par=par, half=half):
                            for k in range(8):
                                dcc = half * 8 + k
                                i = e.transpose(
                                    ps_t3[:, (half * 8 + k) * 128:
                                          (half * 8 + k + 1) * 128],
                                    xn2t[:, par * D + dcc * 128:
                                         par * D + (dcc + 1) * 128],
                                    id_sb[:, :])
                            return i
                        pv = (tt, 0) if half else (tt - 1, 1)
                        tp3_pe[(tt, half)] = prog.add(
                            "tensor", _tp3,
                            deps=[ev_ln2, tp3_ev.get(pv),
                                  pst3_prev.get(half)], inc=PE)

                        def _te3(e, tt=tt, half=half):
                            for k in range(8):
                                dcc = half * 8 + k
                                i = e.tensor_copy(
                                    xn2T[:, dcc * TPC + tt * 128:
                                         dcc * TPC + (tt + 1) * 128],
                                    ps_t3[:, (half * 8 + k) * 128:
                                          (half * 8 + k + 1) * 128])
                            return i
                        tp3_ev[(tt, half)] = prog.add(
                            "vector", _te3, deps=[tp3_pe[(tt, half)]], inc=VE)
                        pst3_prev[half] = tp3_ev[(tt, half)]
                    ev_xn2[tt] = tp3_ev[(tt, 1)]

                for tt in range(NT):
                    par = tt % 2
                    ev_ao = None
                    for dcc in range(DC):
                        ev_ao = dma(
                            "sync", "sAOT%d" % (tt % 2),
                            aor[:, (par * DC + dcc) * 128:(par * DC + dcc + 1) * 128],
                            ao_out[dcc % 2][(dcc // 2) * TPC + tt * 128:
                                            (dcc // 2) * TPC + (tt + 1) * 128, :],
                            deps=[ev_a2a_h[dcc % 2], ev_att_done, ev_onrm_last,
                                  atp_pe.get((tt - 2, 1))])
                    for half in range(2):
                        def _atp(e, par=par, half=half):
                            for k in range(8):
                                dcc = half * 8 + k
                                i = e.transpose(
                                    ps_t3[:, (half * 8 + k) * 128:
                                          (half * 8 + k + 1) * 128],
                                    aor[:, (par * DC + dcc) * 128:
                                        (par * DC + dcc + 1) * 128],
                                    id_sb[:, :])
                            return i
                        atp_pe[(tt, half)] = prog.add(
                            "tensor", _atp,
                            deps=[ev_ao, pst3_prev.get(half),
                                  ev_onrm_last, ev_c_all], inc=PE)

                        def _ate(e, par=par, half=half):
                            return e.tensor_copy(
                                aot[:, par * DC * 128 + half * 1024:
                                    par * DC * 128 + (half + 1) * 1024],
                                ps_t3[:, half * 1024:(half + 1) * 1024])
                        atp_ev[(tt, half)] = prog.add(
                            "vector", _ate,
                            deps=[atp_pe[(tt, half)], ev_pj.get(tt - 2)],
                            inc=VE)
                        pst3_prev[half] = atp_ev[(tt, half)]
                    if tt >= 4:
                        stage3_chain(tt - 4)
                    ev_pjh = {}
                    for h2 in range(2):
                        r3 = (2 * tt + h2) % 3

                        def _pjmm(e, par=par, h2=h2, r3=r3):
                            for dcc in range(DC):
                                for oc2 in range(2):
                                    i = e.matmul(
                                        ps6[:, r3 * 1024 + oc2 * 512:
                                            r3 * 1024 + (oc2 + 1) * 512],
                                        aot[:, (par * DC + dcc) * 128:
                                            (par * DC + dcc + 1) * 128],
                                        wo_sb[:, dcc * D + h2 * 1024 + oc2 * 512:
                                              dcc * D + h2 * 1024 + (oc2 + 1) * 512],
                                        start=(dcc == 0), stop=(dcc == DC - 1))
                            return i
                        ev_pjh[h2] = prog.add(
                            "tensor", _pjmm,
                            deps=[atp_ev[(tt, 0)], atp_ev[(tt, 1)], ev_wo,
                                  ev_att_done,
                                  ev_prjh_ev.get(2 * tt + h2 - 3),
                                  ev_onrm_last],
                            inc=PE)
                        ev_prjh_ev[2 * tt + h2] = prog.add(
                            "vector",
                            lambda e, tt=tt, h2=h2, r3=r3: e.tensor_copy(
                                prjb[:, tt * D + h2 * 1024:tt * D + (h2 + 1) * 1024],
                                ps6[:, r3 * 1024:(r3 + 1) * 1024]),
                            deps=[ev_pjh[h2]], inc=VE)
                    ev_pj[tt] = ev_pjh[1]
                    ev_redb[tt] = prog.add(
                        "vector",
                        lambda e, tt=tt: e.reduce_sum(
                            sum8b[:, tt:tt + 1], prjb[:, tt * D:(tt + 1) * D],
                            AX.X),
                        deps=[ev_prjh_ev[2 * tt + 1]], inc=VE)
                    ev_sqb[tt] = prog.add(
                        "scalar",
                        lambda e, tt=tt: e.activation(
                            sqs[:, :], prjb[:, tt * D:(tt + 1) * D], AF.Square,
                            accum_out=sq8b[:, tt:tt + 1]),
                        deps=[ev_prjh_ev[2 * tt + 1]], inc=SC)
                    if tt == 3:
                        ln1_smalls(0)
                    if tt == 7:
                        ln1_smalls(1)
                    if tt >= 4:
                        stage3_tp(tt - 4)
                for tt in range(4, NT):
                    stage3_chain(tt)
                    if tt >= 5:
                        stage3_tp(tt - 1)
                stage3_tp(NT - 1)
                ev_3a_done = tp3_ev[(NT - 1, 1)]
                prog.flush(block, sems)

            with ExitStack() as st6:
                Ebuf = st6.enter_context(nc.sbuf_tensor("Ebuf", [128, NT * D], F32))
                b1bc_sb = st6.enter_context(nc.sbuf_tensor("b1bc_sb", [128, D], F32))
                tailst = st6.enter_context(nc.sbuf_tensor("tailst", [128, 64], F32))
                bnsc = st6.enter_context(nc.sbuf_tensor("bnsc", [128, 96], F32))
                with ExitStack() as st7:
                    Hq = st7.enter_context(
                        nc.sbuf_tensor("Hq", [128, 2 * PCF * TPC], BF16))
                    w0b = st7.enter_context(nc.sbuf_tensor("w0b", [128, 3 * D], BF16))
                    w1b = st7.enter_context(
                        nc.sbuf_tensor("w1b", [128, 2 * PCF * D], BF16))
                    psx1 = st7.enter_context(nc.psum_tensor("psx1", [128, 512], F32))
                    psx2 = st7.enter_context(nc.psum_tensor("psx2", [128, 512], F32))

                    H_ps = [ps6[:, 0:1024], ps6[:, 1024:2048]]
                    E_ps = [ps6[:, 2048:2560], ps6[:, 2560:3072],
                            psx1[:, :], psx2[:, :]]

                    ev_w0 = {}
                    ev_w1 = {}
                    ev_gelu = {}
                    ev_f2 = {}
                    ev_ea = {}
                    ev_tile_final = {}
                    last_eadd = [None]

                    def sched_w0(fc):
                        gate = ev_gelu.get(fc - 3)
                        assert fc < 3 or gate is not None
                        ev_w0[fc] = dma(
                            "sync" if fc < 3 else "gpsimd",
                            "sW0" + "abc"[fc % 3],
                            w0b[:, (fc % 3) * D:(fc % 3 + 1) * D],
                            w0_out[fc * 128:(fc + 1) * 128, :],
                            deps=[ev_agw0, ev_3a_done, ev_x1wr[6],
                                  ev_x1wr[7], gate])

                    def sched_w1(p):
                        for fi in range(PCF):
                            fc = p * PCF + fi
                            gate1 = ev_f2.get((p - 2, 3, 1))
                            assert p < 2 or gate1 is not None
                            ev_w1[fc] = dma(
                                "sync" if p < 2 else "gpsimd",
                                "sW1" + "ab"[p % 2],
                                w1b[:, ((p % 2) * PCF + fi) * D:
                                    ((p % 2) * PCF + fi + 1) * D],
                                w1_out[fc * 128:(fc + 1) * 128, :],
                                deps=[ev_agw1, ev_3a_done, ev_x1wr[6],
                                      ev_x1wr[7], gate1])

                    def ffn1(p, fi):
                        fc = p * PCF + fi
                        hpar = p % 2
                        par = fc % 2

                        def _f1(e, fc=fc, par=par):
                            for dcc in range(DC):
                                for th in range(2):
                                    i = e.matmul(
                                        H_ps[par][:, th * 512:(th + 1) * 512],
                                        w0b[:, (fc % 3) * D + dcc * 128:
                                            (fc % 3) * D + (dcc + 1) * 128],
                                        xn2T[:, dcc * TPC + th * 512:
                                             dcc * TPC + (th + 1) * 512],
                                        start=(dcc == 0), stop=(dcc == DC - 1))
                            return i
                        ev_mm = prog.add(
                            "tensor", _f1,
                            deps=[ev_w0[fc], ev_3a_done, ev_gelu.get(fc - 2)],
                            inc=PE)
                        ev_gelu[fc] = prog.add(
                            "scalar",
                            lambda e, fc=fc, fi=fi, hpar=hpar, par=par:
                                e.activation(
                                    Hq[:, (hpar * PCF + fi) * TPC:
                                       (hpar * PCF + fi + 1) * TPC],
                                    H_ps[par], AF.Gelu,
                                    bias=b0t_sb[:, fc:fc + 1]),
                            deps=[ev_mm, ev_c_all, ev_f2.get((p - 2, 3, 1))],
                            inc=SC)
                        if fc + 3 < NFC:
                            sched_w0(fc + 3)

                    def ffn2_group(p, oc, ttg):
                        hpar = p % 2
                        for t4 in range(4):
                            tt = ttg * 4 + t4

                            def _f2(e, p=p, oc=oc, tt=tt, t4=t4, hpar=hpar):
                                for fi in range(PCF):
                                    i = e.matmul(
                                        E_ps[t4],
                                        Hq[:, (hpar * PCF + fi) * TPC + tt * 128:
                                           (hpar * PCF + fi) * TPC + (tt + 1) * 128],
                                        w1b[:, ((p % 2) * PCF + fi) * D + oc * 512:
                                            ((p % 2) * PCF + fi) * D + (oc + 1) * 512],
                                        start=(fi == 0), stop=(fi == PCF - 1),
                                        skip_group_check=True)
                                return i
                            ev = prog.add(
                                "tensor", _f2,
                                deps=[ev_gelu[p * PCF + PCF - 1],
                                      ev_w1[p * PCF + PCF - 1],
                                      ev_ea.get(("prev", t4))],
                                inc=PE)
                            ev_f2[(p, oc, ttg)] = ev

                            def _eat(e, p=p, oc=oc, tt=tt, t4=t4):
                                dst = Ebuf[:, tt * D + oc * 512:
                                           tt * D + (oc + 1) * 512]
                                if p == 0:
                                    return e.tensor_tensor(
                                        dst, E_ps[t4],
                                        b1bc_sb[:, oc * 512:(oc + 1) * 512],
                                        OP.add)
                                return e.tensor_tensor(dst, E_ps[t4], dst, OP.add)
                            ev_ea[("prev", t4)] = prog.add(
                                "vector", _eat,
                                deps=[ev] + ([ev_b1bc] if p == 0 else []),
                                inc=VE)
                            last_eadd[0] = ev_ea[("prev", t4)]
                            if p == NPC - 1 and oc == 3:
                                ev_tile_final[tt] = ev_ea[("prev", t4)]

                    ev_b1bc = dma("sync", "sC", b1bc_sb[:, :], b1bc[:, :],
                                  deps=[ev_3a_done, ev_x1wr[6], ev_x1wr[7]])

                    for fc in range(3):
                        sched_w0(fc)
                    for fi in range(PCF):
                        ffn1(0, fi)
                    sched_w1(0)
                    sched_w1(1)
                    for p in range(NPC - 1):
                        gi = 0
                        for oc in range(4):
                            for ttg in range(2):
                                ffn2_group(p, oc, ttg)
                                if gi < PCF:
                                    ffn1(p + 1, gi)
                                gi += 1
                        if p + 2 < NPC:
                            sched_w1(p + 2)

                    with ExitStack() as st8:
                        x1r = st8.enter_context(
                            nc.sbuf_tensor("x1r", [128, 2 * D], F32))

                        ev_x1l = {}
                        ev_ywr = {}

                        def x1_load(tt):
                            deps = [ev_gelu[NFC - 1], ev_3a_done,
                                    ev_x1wr[6], ev_x1wr[7]]
                            if tt >= 2:
                                deps.append(ev_ywr[tt - 2])
                            ev_x1l[tt] = dma(
                                "sync", "sX1R%d" % (tt % 2),
                                x1r[:, (tt % 2) * D:(tt % 2 + 1) * D],
                                x1_sp[tt * 128:(tt + 1) * 128, :],
                                deps=deps)

                        ev_tail_rs = {}

                        def tail_stats_group(tg):
                            base = tg * 16
                            u4 = tailst[:, base + 0:base + 4]
                            rstd4 = tailst[:, base + 4:base + 8]
                            nr4 = tailst[:, base + 8:base + 12]
                            mv = tailst[:, 32 + tg * 8:32 + tg * 8 + 8]

                            ev_ag = None
                            for t4 in range(4):
                                tt = tg * 4 + t4

                                def _bn(e, tt=tt, t4=t4):
                                    for c in range(4):
                                        i = e.bn_stats(
                                            bnsc[:, (t4 * 4 + c) * 6:
                                                 (t4 * 4 + c + 1) * 6],
                                            Ebuf[:, tt * D + c * 512:
                                                 tt * D + (c + 1) * 512])
                                    return i
                                prog.add("vector", _bn,
                                         deps=[ev_tile_final[tt]], inc=VE)
                                ev_ag = prog.add(
                                    "vector",
                                    lambda e, t4=t4: e.bn_aggr(
                                        mv[:, t4 * 2:t4 * 2 + 2],
                                        bnsc[:, t4 * 24:(t4 + 1) * 24]),
                                    inc=VE)
                            mean4 = mv[:, 0:8:2]
                            var4 = mv[:, 1:8:2]
                            prog.add("scalar",
                                     lambda e, u4=u4, var4=var4: e.activation(
                                         u4, var4, AF.Ln, bias=eps_sb[:, :]),
                                     deps=[ev_ag], inc=SC)
                            ev_rs = prog.add(
                                "scalar",
                                lambda e, rstd4=rstd4, u4=u4: e.activation(
                                    rstd4, u4, AF.Exp, scale=-0.5),
                                inc=SC)
                            ev_tail_rs[tg] = prog.add(
                                "vector",
                                lambda e, nr4=nr4, mean4=mean4, rstd4=rstd4:
                                    e.scalar_tensor_tensor(
                                        nr4, mean4, -1.0, rstd4,
                                        OP.mult, OP.mult),
                                deps=[ev_rs], inc=VE)

                        def tail_apply(tt):
                            tg, t4 = tt // 4, tt % 4
                            base = tg * 16
                            nr1 = tailst[:, base + 8 + t4:base + 9 + t4]
                            rstd1 = tailst[:, base + 4 + t4:base + 5 + t4]
                            Et = Ebuf[:, tt * D:(tt + 1) * D]
                            x1sl = x1r[:, (tt % 2) * D:(tt % 2 + 1) * D]
                            ev_ap = prog.add(
                                "scalar",
                                lambda e, Et=Et, nr1=nr1, rstd1=rstd1:
                                    e.activation(Et, Et, AF.Identity,
                                                 bias=nr1, scale=rstd1),
                                deps=[ev_tail_rs[tg]], inc=SC)
                            ev_fin = prog.add(
                                "vector",
                                lambda e, x1sl=x1sl, Et=Et: e.tensor_tensor(
                                    x1sl, Et, x1sl, OP.add),
                                deps=[ev_ap, ev_x1l[tt]], inc=VE)
                            ev_ywr[tt] = dma(
                                "sync", "sY%d" % (tt % 2),
                                y[tt * 128:(tt + 1) * 128, :], x1sl,
                                deps=[ev_fin])

                        x1_load(0)
                        x1_load(1)
                        for oc in range(4):
                            ffn2_group(NPC - 1, oc, 0)
                        tail_stats_group(0)
                        for oc in range(4):
                            ffn2_group(NPC - 1, oc, 1)
                            tail_apply(oc)
                            x1_load(oc + 2)
                        tail_stats_group(1)
                        for tt in range(4, NT):
                            tail_apply(tt)
                            if tt + 2 < NT:
                                x1_load(tt + 2)

                        prog.add("sync", lambda e: None,
                                 deps=[ev_ywr[NT - 2], ev_ywr[NT - 1]])
                        prog.flush(block, sems)

    return nc


def _stage_inputs(x, wp, wo, W0, b0, W1, b1):
    x = np.ascontiguousarray(np.asarray(x, dtype=np.float32).reshape(S, D))
    wp = np.asarray(wp, dtype=np.float32)
    wo = np.asarray(wo, dtype=np.float32)
    W0 = np.asarray(W0, dtype=np.float32)
    b0 = np.asarray(b0, dtype=np.float32)
    W1 = np.asarray(W1, dtype=np.float32)
    b1 = np.asarray(b1, dtype=np.float32)

    wp5 = wp.reshape(3, 16, 128, DC, 128)
    woT = np.ascontiguousarray(wo.T)
    W0r = W0.reshape(NFC, 128, DC, 128)
    W1T = np.ascontiguousarray(W1.T)
    b0t = np.ascontiguousarray(b0.reshape(NFC, 128).T)
    b1bc = np.ascontiguousarray(np.broadcast_to(b1.reshape(1, D), (128, D)))

    kq = np.arange(128)
    tri = np.where(kq[:, None] <= kq[None, :], 0.0, NEG).astype(np.float32)
    ident = np.eye(128, dtype=ml_dtypes.bfloat16)

    in_maps = []
    for c in range(NCORES):
        blk = wp5[:, 2 * c:2 * c + 2]
        blk = np.transpose(blk, (4, 1, 0, 3, 2))
        wq_im = np.ascontiguousarray(blk.reshape(128, 96 * 128))

        w0s = W0r[8 * c:8 * c + 8]
        w0s = np.ascontiguousarray(
            np.transpose(w0s, (0, 3, 2, 1)).reshape(1024, D))

        in_maps.append({
            "xs": np.ascontiguousarray(x[TPC * c:TPC * (c + 1)]),
            "wqkv": wq_im,
            "wo_sh": np.ascontiguousarray(woT[256 * c:256 * (c + 1)]),
            "w0_sh": w0s,
            "w1_sh": np.ascontiguousarray(W1T[1024 * c:1024 * (c + 1)]),
            "b0t": b0t,
            "b1bc": b1bc,
            "tri": tri,
            "identd": ident,
        })
    return in_maps


def kernel(L0_attn_x, L0_attn_w_project, L0_attn_w_out,
           L0_ffn_W0, L0_ffn_bias0, L0_ffn_W1, L0_ffn_bias1):
    in_maps = _stage_inputs(L0_attn_x, L0_attn_w_project, L0_attn_w_out,
                            L0_ffn_W0, L0_ffn_bias0, L0_ffn_W1, L0_ffn_bias1)
    nc = build_nc()
    res = run_bass_kernel_spmd(nc, in_maps, core_ids=list(range(NCORES)))
    out = np.concatenate([np.asarray(res.results[c]["y"])
                          for c in range(NCORES)], axis=0)
    return out.reshape(1, S, D).astype(np.float32)


# version 20
# speedup vs baseline: 1.0440x; 1.0440x over previous
import math
from collections import defaultdict
from contextlib import ExitStack

import numpy as np
import ml_dtypes

import concourse.bass as bass
import concourse.mybir as mybir
from concourse.bass_utils import run_bass_kernel_spmd
from concourse.aot_env import allow_ici_all_to_all

F32 = mybir.dt.float32
BF16 = mybir.dt.bfloat16
AF = mybir.ActivationFunctionType
AX = mybir.AxisListType
OP = mybir.AluOpType

NCORES = 8
S = 8192
D = 2048
HD = 128
TPC = 1024
NT = 8
DC = 16
FF = 8192
NFC = 64
NPC = 16
PCF = 4
EPS = 1e-5
ISQ = 1.0 / math.sqrt(HD)
NEG = -1.0e9

ENGS = ("sync", "tensor", "scalar", "vector", "gpsimd")
OWN = {"sP": "tensor"}


class Prog:

    def __init__(self):
        self.steps = {e: [] for e in ENGS}
        self.cnt = defaultdict(int)
        self.waited = {e: defaultdict(int) for e in ENGS}
        self.last_ev = {}

    def add(self, eng, emit, deps=(), inc=None, n=1, chain=True):
        deps = list(deps)
        if chain and eng in ("vector", "scalar"):
            deps.append(self.last_ev.get(eng))
        ev = None
        if inc is not None:
            self.cnt[inc] += n
            ev = (inc, self.cnt[inc])
            if chain and eng in ("vector", "scalar"):
                self.last_ev[eng] = ev
        deps = tuple(d for d in deps if d is not None)
        self.steps[eng].append((emit, deps, inc, n))
        return ev

    def flush(self, block, sems):
        for name in ENGS:
            if not self.steps[name]:
                continue
            steps = self.steps[name]
            self.steps[name] = []
            waited = self.waited[name]

            def body(eng, steps=steps, waited=waited, name=name):
                for emit, deps, inc, n in steps:
                    for (s_, v) in deps:
                        if OWN.get(s_) == name:
                            continue
                        if waited[s_] < v:
                            eng.wait_ge(sems[s_], v)
                            waited[s_] = v
                    inst = emit(eng)
                    if inc is not None:
                        inst.then_inc(sems[inc], n)
            getattr(block, name)(body)


def build_nc():
    nc = bass.Bass("TRN2", target_bir_lowering=False, debug=False)

    xs = nc.dram_tensor("xs", [TPC, D], F32, kind="ExternalInput")
    wqkv = nc.dram_tensor("wqkv", [128, 96 * 128], F32, kind="ExternalInput")
    wo_sh = nc.dram_tensor("wo_sh", [256, D], F32, kind="ExternalInput")
    w0_sh = nc.dram_tensor("w0_sh", [1024, D], F32, kind="ExternalInput")
    w1_sh = nc.dram_tensor("w1_sh", [1024, D], F32, kind="ExternalInput")
    b0t = nc.dram_tensor("b0t", [128, NFC], F32, kind="ExternalInput")
    b1bc = nc.dram_tensor("b1bc", [128, D], F32, kind="ExternalInput")
    tri = nc.dram_tensor("tri", [128, 128], F32, kind="ExternalInput")
    identd = nc.dram_tensor("identd", [128, 128], BF16, kind="ExternalInput")
    y = nc.dram_tensor("y", [TPC, D], F32, kind="ExternalOutput")

    xn_in = [nc.dram_tensor("xn_in%d" % i, [D, TPC // 4], BF16, kind="Internal")
             for i in range(4)]
    xn_out = [nc.dram_tensor("xn_out%d" % i, [NCORES * D, TPC // 4], BF16,
                             kind="Internal", addr_space="Shared")
              for i in range(4)]
    wo_in = nc.dram_tensor("wo_in", [256, D], BF16, kind="Internal")
    wo_out = nc.dram_tensor("wo_out", [D, D], BF16, kind="Internal",
                            addr_space="Shared")
    w0_in = nc.dram_tensor("w0_in", [1024, D], BF16, kind="Internal")
    w0_out = nc.dram_tensor("w0_out", [FF, D], BF16, kind="Internal",
                            addr_space="Shared")
    w1_in = nc.dram_tensor("w1_in", [1024, D], BF16, kind="Internal")
    w1_out = nc.dram_tensor("w1_out", [FF, D], BF16, kind="Internal",
                            addr_space="Shared")
    ao_in = [nc.dram_tensor("ao_in%d" % h, [S, 128], BF16, kind="Internal")
             for h in range(2)]
    ao_out = [nc.dram_tensor("ao_out%d" % h, [S, 128], BF16, kind="Internal")
              for h in range(2)]
    x1_sp = nc.dram_tensor("x1_sp", [TPC, D], F32, kind="Internal")

    rg = [list(range(NCORES))]
    prog = Prog()

    with ExitStack() as st:
        b0t_sb = st.enter_context(nc.sbuf_tensor("b0t_sb", [128, NFC], F32))
        tri_sb = st.enter_context(nc.sbuf_tensor("tri_sb", [128, 128], F32))
        id_sb = st.enter_context(nc.sbuf_tensor("id_sb", [128, 128], BF16))
        eps_sb = st.enter_context(nc.sbuf_tensor("eps_sb", [128, 1], F32))
        stats = st.enter_context(nc.sbuf_tensor("stats", [128, 16], F32))
        sqs = st.enter_context(nc.sbuf_tensor("sqs", [128, D], F32))
        ps6 = st.enter_context(nc.psum_tensor("ps6", [128, 3072], F32))

        sems = {}
        for s_ in ("sP", "sA", "sV",
                   "sC", "sX", "sXq1", "sXq2", "sXN", "sXN1", "sAO0", "sAO1",
                   "sWB", "sWO", "sXG",
                   "sRHS00", "sRHS01", "sRHS10", "sRHS11",
                   "sRHG00", "sRHG01", "sRHG10", "sRHG11", "sW0a", "sW0b", "sW0c", "sW0d", "sW1a", "sW1b",
                   "sXS0", "sXS1", "sAOT0", "sAOT1", "sX1W0", "sX1W1",
                   "sX1R0", "sX1R1", "sY0", "sY1",
                   "ccX", "ccWo", "ccW0", "ccW1", "ccAO"):
            sems[s_] = st.enter_context(nc.semaphore(s_))
        PE, SC, VE = "sP", "sA", "sV"

        block = st.enter_context(nc.Block())

        def dma(eng_name, sem, out, in_, deps=(), chain=True):
            return prog.add(eng_name,
                            lambda e, o=out, i=in_: e.dma_start(out=o, in_=i),
                            deps=deps, inc=sem, n=16, chain=chain)

        ln_par = [0]

        def ln(tile_ap, out_ap, ev_tile, extra_apply=()):
            p8 = (ln_par[0] % 2) * 8
            ln_par[0] += 1
            ssum = stats[:, p8 + 0:p8 + 1]
            negm = stats[:, p8 + 1:p8 + 2]
            m = stats[:, p8 + 2:p8 + 3]
            msq = stats[:, p8 + 3:p8 + 4]
            var = stats[:, p8 + 4:p8 + 5]
            u = stats[:, p8 + 5:p8 + 6]
            rstd = stats[:, p8 + 6:p8 + 7]
            sumsq = stats[:, p8 + 7:p8 + 8]

            ev_r = prog.add("vector",
                            lambda e: e.reduce_sum(ssum, tile_ap, AX.X),
                            deps=[ev_tile], inc=VE)
            ev_q = prog.add("scalar",
                            lambda e: e.activation(sqs[:, :], tile_ap, AF.Square,
                                                   accum_out=sumsq),
                            deps=[ev_tile], inc=SC)

            def _m(e):
                e.tensor_scalar_mul(m, ssum, 1.0 / D)
                return e.tensor_scalar_mul(negm, ssum, -1.0 / D)
            ev_m = prog.add("vector", _m, deps=[ev_r], inc=VE)
            ev_mq = prog.add("vector",
                             lambda e: e.tensor_tensor(msq, m, m, OP.mult),
                             deps=[ev_m], inc=VE)
            ev_v = prog.add("vector",
                            lambda e: e.scalar_tensor_tensor(
                                var, sumsq, 1.0 / D, msq, OP.mult, OP.subtract),
                            deps=[ev_q, ev_mq], inc=VE)

            ev_u = prog.add("scalar",
                            lambda e: e.activation(u, var, AF.Ln,
                                                   bias=eps_sb[:, :]),
                            deps=[ev_v], inc=SC)
            ev_rs = prog.add("scalar",
                             lambda e: e.activation(rstd, u, AF.Exp, scale=-0.5),
                             deps=[ev_u], inc=SC)

            return prog.add("vector",
                            lambda e: e.tensor_scalar(out_ap, tile_ap, negm, rstd,
                                                      OP.add, OP.mult),
                            deps=[ev_rs, *extra_apply], inc=VE)

        stW = st.enter_context(ExitStack())
        wq_sb = stW.enter_context(nc.sbuf_tensor("wq_sb", [128, 96 * 128], BF16))

        ev_c_b0 = dma("sync", "sC", b0t_sb[:, :], b0t[:, :])
        ev_c_tri = dma("sync", "sC", tri_sb[:, :], tri[:, :])
        ev_c_id = dma("sync", "sC", id_sb[:, :], identd[:, :])
        ev_c_all = ev_c_id
        prog.add("vector", lambda e: e.memset(eps_sb[:, :], EPS), inc=VE)


        prog.flush(block, sems)

        with ExitStack() as st1:
            xall = st1.enter_context(nc.sbuf_tensor("xall", [128, NT * D], F32))
            xn_t = st1.enter_context(nc.sbuf_tensor("xn_t", [128, 2 * D], BF16))
            xnT = st1.enter_context(nc.sbuf_tensor("xnT", [128, DC * TPC], BF16))
            ps_t = st1.enter_context(nc.psum_tensor("ps_t", [128, 2048], BF16))
            lnst = st1.enter_context(nc.sbuf_tensor("lnst", [128, 64], F32))

            qeng = ["sync", "scalar", "gpsimd", "gpsimd"]
            qsem = ["sX", "sXq1", "sXq2", "sXG"]
            ev_xq = []
            for q in range(4):
                srcap = xs.ap()[q * 256:(q + 1) * 256, :]
                srcap = srcap.rearrange("(tt p) d -> p tt d", p=128)
                dstap = xall[:, q * 2 * D:(q + 1) * 2 * D]
                dstap = dstap.rearrange("p (tt d) -> p tt d", d=D)
                ev_xq.append(dma(qeng[q], qsem[q], dstap, srcap, chain=False))
            ev_wq = dma("gpsimd", "sWB", wq_sb[:, :], wqkv[:, :])

            sum8 = lnst[:, 0:8]
            sq8 = lnst[:, 8:16]
            m8 = lnst[:, 16:24]
            negm8 = lnst[:, 24:32]
            msq8 = lnst[:, 32:40]
            var8 = lnst[:, 40:48]
            u8 = lnst[:, 48:56]
            rstd8 = lnst[:, 56:64]

            tp_pe = {}
            tp_ev = {}
            ev_agx = {}
            for hh in range(2):
                tl = slice(4 * hh, 4 * hh + 4)
                hdeps = [ev_xq[2 * hh], ev_xq[2 * hh + 1]]
                ev_red = prog.add(
                    "vector",
                    lambda e, tl=tl, hh=hh: e.reduce_sum(
                        sum8[:, tl],
                        xall[:, hh * 4 * D:(hh + 1) * 4 * D].rearrange(
                            "p (t d) -> p t d", d=D),
                        AX.X),
                    deps=hdeps, inc=VE)
                ev_sq = None
                for tt in range(4 * hh, 4 * hh + 4):
                    ev_sq = prog.add(
                        "scalar",
                        lambda e, tt=tt: e.activation(
                            sqs[:, :], xall[:, tt * D:(tt + 1) * D], AF.Square,
                            accum_out=sq8[:, tt:tt + 1]),
                        deps=[ev_xq[tt // 2]], inc=SC)
                prog.add("vector",
                         lambda e, tl=tl: e.tensor_scalar_mul(
                             m8[:, tl], sum8[:, tl], 1.0 / D),
                         deps=[ev_red], inc=VE)
                prog.add("vector",
                         lambda e, tl=tl: e.tensor_scalar_mul(
                             negm8[:, tl], sum8[:, tl], -1.0 / D),
                         inc=VE)
                prog.add("vector",
                         lambda e, tl=tl: e.tensor_tensor(
                             msq8[:, tl], m8[:, tl], m8[:, tl], OP.mult),
                         inc=VE)
                ev_v8 = prog.add(
                    "vector",
                    lambda e, tl=tl: e.scalar_tensor_tensor(
                        var8[:, tl], sq8[:, tl], 1.0 / D, msq8[:, tl],
                        OP.mult, OP.subtract),
                    deps=[ev_sq], inc=VE)
                prog.add("scalar",
                         lambda e, tl=tl: e.activation(
                             u8[:, tl], var8[:, tl], AF.Ln, bias=eps_sb[:, :]),
                         deps=[ev_v8], inc=SC)
                ev_r8h = prog.add(
                    "scalar",
                    lambda e, tl=tl: e.activation(
                        rstd8[:, tl], u8[:, tl], AF.Exp, scale=-0.5),
                    inc=SC)

                for tt in range(4 * hh, 4 * hh + 4):
                    xtile = xall[:, tt * D:(tt + 1) * D]
                    xnt = xn_t[:, (tt % 2) * D:(tt % 2 + 1) * D]
                    ev_ap = prog.add(
                        "vector",
                        lambda e, xtile=xtile, xnt=xnt, tt=tt: e.tensor_scalar(
                            xnt, xtile, negm8[:, tt:tt + 1], rstd8[:, tt:tt + 1],
                            OP.add, OP.mult),
                        deps=[ev_r8h, tp_pe.get((tt - 2, 1))], inc=VE)
                    for half in range(2):
                        def _tp(e, xnt=xnt, half=half):
                            for k in range(8):
                                dcc = half * 8 + k
                                i = e.transpose(
                                    ps_t[:, (half * 8 + k) * 128:(half * 8 + k + 1) * 128],
                                    xnt[:, dcc * 128:(dcc + 1) * 128], id_sb[:, :])
                            return i
                        pv = (tt, 0) if half else (tt - 1, 1)
                        tp_pe[(tt, half)] = prog.add(
                            "tensor", _tp,
                            deps=[ev_ap, ev_c_all, tp_ev.get(pv)], inc=PE)

                        def _te(e, tt=tt, half=half):
                            for k in range(8):
                                dcc = half * 8 + k
                                i = e.tensor_copy(
                                    xnT[:, dcc * TPC + tt * 128:dcc * TPC + (tt + 1) * 128],
                                    ps_t[:, (half * 8 + k) * 128:(half * 8 + k + 1) * 128])
                            return i
                        tp_ev[(tt, half)] = prog.add("vector", _te,
                                                     deps=[tp_pe[(tt, half)]], inc=VE)

                for q in (2 * hh, 2 * hh + 1):
                    gate = tp_ev[(2 * q + 1, 1)]
                    ev_wr = None
                    for dcc in range(DC):
                        ev_wr = dma("sync", "sXN" if q % 2 == 0 else "sXN1",
                                    xn_in[q][dcc * 128:(dcc + 1) * 128, :],
                                    xnT[:, dcc * TPC + q * 256:
                                        dcc * TPC + q * 256 + 256],
                                    deps=[gate])
                    ev_agx[q] = prog.add(
                        "gpsimd",
                        lambda e, q=q: e.collective_compute(
                            "AllGather", OP.bypass, replica_groups=rg,
                            ins=[xn_in[q].ap().opt()],
                            outs=[xn_out[q].ap().opt()]),
                        deps=[ev_wr, ev_agx.get(q - 1)], inc="ccX",
                        n=1)

            prog.flush(block, sems)

        with ExitStack() as stB:
            qsb = stB.enter_context(nc.sbuf_tensor("qsb", [128, 2 * S], BF16))
            ksb = stB.enter_context(nc.sbuf_tensor("ksb", [128, 2 * S], BF16))
            vsb = stB.enter_context(nc.sbuf_tensor("vsb", [128, 2 * S], BF16))

            with ExitStack() as st2:
                rhs_sb = st2.enter_context(
                    nc.sbuf_tensor("rhs_sb", [128, 2 * DC * 1024], BF16))
                dest = {0: qsb, 1: ksb, 2: vsb}

                ev_rhs = {}
                pj_pe = {}
                pj_ev = {}
                slot_prev = {}
                ci = 0
                for hpass in range(2):
                    for pr in range(8):
                        slot = pr % 2
                        evs = []
                        for qh in range(2):
                            for q in (2 * hpass, 2 * hpass + 1):
                                srcap = xn_out[q].ap()[
                                    2048 * pr + qh * 1024:
                                    2048 * pr + (qh + 1) * 1024, :]
                                srcap = srcap.rearrange("(dc p) t -> p dc t",
                                                        p=128)
                                dst = rhs_sb[:, (slot * DC + qh * 8) * 1024:
                                             (slot * DC + (qh + 1) * 8) * 1024]
                                dst = dst.rearrange("p (dc t2) -> p dc t2",
                                                    t2=1024)
                                dst = dst[:, :, q * 256:(q + 1) * 256]
                                eng = "sync" if qh == 0 else "gpsimd"
                                sem = "sRH%s%d%d" % ("SG"[qh], slot, q % 2)
                                evs.append(dma(eng, sem, dst, srcap,
                                               deps=[ev_agx[q],
                                                     slot_prev.get(slot)]))
                        ev_rhs[(hpass, pr)] = evs

                        for b in range(6):
                            tc = 2 * pr + hpass
                            psl = ps6[:, (ci % 6) * 512:(ci % 6 + 1) * 512]

                            def _mm(e, b=b, half=hpass, psl=psl, slot=slot):
                                for dcc in range(DC):
                                    i = e.matmul(
                                        psl,
                                        wq_sb[:, (b * DC + dcc) * 128:
                                              (b * DC + dcc + 1) * 128],
                                        rhs_sb[:, (slot * DC + dcc) * 1024
                                               + half * 512:
                                               (slot * DC + dcc) * 1024
                                               + half * 512 + 512],
                                        start=(dcc == 0), stop=(dcc == DC - 1))
                                return i
                            pj_pe[ci] = prog.add(
                                "tensor", _mm,
                                deps=evs + [ev_wq, pj_ev.get(ci - 6)],
                                inc=PE)

                            h_l, ty = b // 3, b % 3
                            dsb = dest[ty]
                            pj_ev[ci] = prog.add(
                                "vector",
                                lambda e, dsb=dsb, h_l=h_l, tc=tc, psl=psl:
                                    e.tensor_copy(
                                        dsb[:, h_l * S + tc * 512:
                                            h_l * S + (tc + 1) * 512],
                                        psl),
                                deps=[pj_pe[ci]], inc=VE)
                            ci += 1
                        slot_prev[slot] = pj_pe[ci - 1]

                ev_proj_done = pj_ev[ci - 1]

                ev_bn = dma("gpsimd", "sWB", wo_in[:, :], wo_sh[:, :],
                            deps=[ev_agx[3]])
                ev_bn = dma("gpsimd", "sWB", w0_in[:, :], w0_sh[:, :])
                ev_bn = dma("gpsimd", "sWB", w1_in[:, :], w1_sh[:, :])
                ev_agwo = prog.add("gpsimd",
                                   lambda e: e.collective_compute(
                                       "AllGather", OP.bypass, replica_groups=rg,
                                       ins=[wo_in.ap().opt()], outs=[wo_out.ap().opt()]),
                                   deps=[ev_bn, ev_rhs[(1, 7)][1], ev_rhs[(1, 7)][3]],
                                   inc="ccWo")
                ev_agw0 = prog.add("gpsimd",
                                   lambda e: e.collective_compute(
                                       "AllGather", OP.bypass, replica_groups=rg,
                                       ins=[w0_in.ap().opt()], outs=[w0_out.ap().opt()]),
                                   deps=[ev_agwo], inc="ccW0")
                ev_agw1 = prog.add("gpsimd",
                                   lambda e: e.collective_compute(
                                       "AllGather", OP.bypass, replica_groups=rg,
                                       ins=[w1_in.ap().opt()], outs=[w1_out.ap().opt()]),
                                   deps=[ev_agw0], inc="ccW1")

                prog.flush(block, sems)

            stV = stB.enter_context(ExitStack())
            vsb2 = stV.enter_context(
                nc.sbuf_tensor("vsb2", [128, 2 * 64 * 129], BF16))
            with ExitStack() as st3:
                ps_t2 = st3.enter_context(
                    nc.psum_tensor("ps_t2", [128, 2048], BF16))
                ev_ones = prog.add(
                    "vector",
                    lambda e: e.memset(
                        vsb2[:, :].rearrange(
                            "p (c w) -> p c w", w=129)[:, :, 128:129],
                        1.0),
                    deps=[ev_proj_done], inc=VE)
                vt_ev = {}
                bidx = 0
                for h_l in range(2):
                    for bb in range(8):
                        half = bidx % 2

                        def _vt(e, h_l=h_l, bb=bb, half=half):
                            for k in range(8):
                                kcc = bb * 8 + k
                                i = e.transpose(
                                    ps_t2[:, (half * 8 + k) * 128:
                                          (half * 8 + k + 1) * 128],
                                    vsb[:, h_l * S + kcc * 128:
                                        h_l * S + (kcc + 1) * 128],
                                    id_sb[:, :])
                            return i
                        ev_vt = prog.add("tensor", _vt,
                                         deps=[ev_proj_done,
                                               vt_ev.get(bidx - 2)], inc=PE)

                        def _ve(e, h_l=h_l, bb=bb, half=half):
                            for k in range(8):
                                kcc = bb * 8 + k
                                i = e.tensor_copy(
                                    vsb2[:, (h_l * 64 + kcc) * 129:
                                         (h_l * 64 + kcc) * 129 + 128],
                                    ps_t2[:, (half * 8 + k) * 128:
                                          (half * 8 + k + 1) * 128])
                            return i
                        vt_ev[bidx] = prog.add("vector", _ve,
                                               deps=[ev_vt, ev_ones], inc=VE)
                        bidx += 1
                ev_vt_done = vt_ev[bidx - 1]
                prog.flush(block, sems)

            with ExitStack() as st4:
                ps_o = st4.enter_context(nc.psum_tensor("ps_o", [128, 512], F32))
                pt_sb = st4.enter_context(
                    nc.sbuf_tensor("pt_sb", [128, 3 * 1024], BF16))
                aoev = st4.enter_context(
                    nc.sbuf_tensor("aoev", [128, 2 * 1024], BF16))
                rls = st4.enter_context(nc.sbuf_tensor("rls", [128, 16], F32))

                S_ps = [ps6[:, 0:1024], ps6[:, 1024:2048]]

                def otile(t4):
                    if t4 < 6:
                        base = 2048 + (t4 // 3) * 512 + (t4 % 3) * 129
                        return ps6[:, base:base + 129]
                    return ps_o[:, (t4 - 6) * 129:(t4 - 6) * 129 + 129]

                gstep = 0
                hg = 0
                ev_exp = {}
                ev_av = {}
                ev_onrm = {}
                ev_aowr = {}

                def plan_kc(h_l, g, kc):
                    diag = kc >= 8 * g
                    c0 = 128 * (kc - 8 * g) if diag else 0
                    pieces = ([(c0, 512), (512, 1024)] if c0 < 512
                              else [(c0, 1024)])
                    return diag, c0, tuple(pieces)

                def emit_s_exp(h_l, g, kc, gs):
                    diag, c0, pieces = plan_kc(h_l, g, kc)
                    spar = gs % 2
                    Sp = S_ps[spar]
                    pt = pt_sb[:, (gs % 3) * 1024:(gs % 3 + 1) * 1024]

                    def _smm(e, h_l=h_l, g=g, kc=kc, pieces=pieces, Sp=Sp):
                        for (a, b_) in pieces:
                            i = e.matmul(
                                Sp[:, a:b_],
                                ksb[:, h_l * S + kc * 128:h_l * S + (kc + 1) * 128],
                                qsb[:, h_l * S + g * 1024 + a:
                                    h_l * S + g * 1024 + b_],
                                start=True, stop=True)
                        return i
                    ev_s = prog.add("tensor", _smm,
                                    deps=[ev_vt_done, ev_exp.get(gs - 2)], inc=PE)
                    dep_exp = ev_s
                    if diag:
                        dep_exp = prog.add(
                            "vector",
                            lambda e, Sp=Sp, c0=c0:
                                e.tensor_tensor(Sp[:, c0:c0 + 128],
                                                Sp[:, c0:c0 + 128],
                                                tri_sb[:, :], OP.add),
                            deps=[ev_s, ev_c_all], inc=VE)
                    ev_exp[gs] = prog.add(
                        "scalar",
                        lambda e, Sp=Sp, pt=pt, c0=c0:
                            e.activation(pt[:, c0:1024], Sp[:, c0:1024],
                                         AF.Exp, scale=ISQ),
                        deps=[dep_exp, ev_av.get(gs - 3)], inc=SC)

                def emit_av(h_l, g, kc, gs, hg):
                    diag, c0, pieces = plan_kc(h_l, g, kc)
                    t4min = c0 // 128
                    pt = pt_sb[:, (gs % 3) * 1024:(gs % 3 + 1) * 1024]

                    def _av(e, h_l=h_l, kc=kc, t4min=t4min, pt=pt,
                            first=(kc == 0)):
                        for t4 in range(t4min, 8):
                            i = e.matmul(
                                otile(t4),
                                pt[:, t4 * 128:(t4 + 1) * 128],
                                vsb2[:, (h_l * 64 + kc) * 129:
                                     (h_l * 64 + kc) * 129 + 129],
                                start=(first and t4 in (0, 3, 6)), stop=False,
                                skip_group_check=True)
                        return i
                    deps_a = [ev_exp[gs]]
                    if kc == 0:
                        deps_a.append(ev_onrm.get(hg - 1))
                    ev_av[gs] = prog.add("tensor", _av, deps=deps_a, inc=PE)

                ev_a2a_h = {}
                for h_l in range(2):
                    for g in range(8):
                        nkc = 8 * (g + 1)
                        for kc in range(nkc + 1):
                            if kc < nkc:
                                emit_s_exp(h_l, g, kc, gstep + kc)
                            if kc >= 1:
                                emit_av(h_l, g, kc - 1, gstep + kc - 1, hg)
                        gstep += nkc

                        def _rc(e, hg=hg):
                            for t4 in range(8):
                                i = e.reciprocal(
                                    rls[:, (hg % 2) * 8 + t4:
                                        (hg % 2) * 8 + t4 + 1],
                                    otile(t4)[:, 128:129])
                            return i
                        ev_rc = prog.add("vector", _rc,
                                         deps=[ev_av[gstep - 1]], inc=VE)

                        def _nm(e, hg=hg):
                            for t4 in range(8):
                                i = e.tensor_scalar_mul(
                                    aoev[:, (hg % 2) * 1024 + t4 * 128:
                                         (hg % 2) * 1024 + (t4 + 1) * 128],
                                    otile(t4)[:, 0:128],
                                    rls[:, (hg % 2) * 8 + t4:
                                        (hg % 2) * 8 + t4 + 1])
                            return i
                        ev_onrm[hg] = prog.add(
                            "vector", _nm,
                            deps=[ev_rc, ev_aowr.get(hg - 2)], inc=VE)

                        src = aoev[:, (hg % 2) * 1024:(hg % 2 + 1) * 1024]
                        src = src.rearrange("q (t4 d) -> q t4 d", d=128)
                        dst = ao_in[h_l].ap()[TPC * g:TPC * (g + 1), :]
                        dst = dst.rearrange("(t4 q) d -> q t4 d", q=128)
                        ev_aowr[hg] = dma("sync", "sAO%d" % (hg % 2), dst, src,
                                          deps=[ev_onrm[hg]])
                        hg += 1

                    def _a2a(e, h_l=h_l):
                        with allow_ici_all_to_all():
                            return e.collective_compute(
                                "AllToAll", OP.bypass, replica_groups=rg,
                                ins=[ao_in[h_l].ap().opt()],
                                outs=[ao_out[h_l].ap().opt()])
                    ev_a2a_h[h_l] = prog.add(
                        "gpsimd", _a2a,
                        deps=[ev_aowr[hg - 1], ev_aowr[hg - 2],
                              ev_a2a_h.get(0) if h_l else None],
                        inc="ccAO")

                ev_att_done = ev_av[gstep - 1]
                ev_onrm_last = ev_onrm[hg - 1]
                prog.flush(block, sems)

        stW.close()

        with ExitStack() as stC:
            xn2T = stC.enter_context(nc.sbuf_tensor("xn2T", [128, DC * TPC], BF16))

            with ExitStack() as st5:
                wo_sb = st5.enter_context(nc.sbuf_tensor("wo_sb", [128, DC * D], BF16))
                aot = st5.enter_context(
                    nc.sbuf_tensor("aot", [128, 2 * DC * 128], BF16))
                aor = st5.enter_context(
                    nc.sbuf_tensor("aor", [128, 2 * DC * 128], BF16))
                prjb = st5.enter_context(nc.sbuf_tensor("prjb", [128, NT * D], BF16))
                x1t = st5.enter_context(nc.sbuf_tensor("x1t", [128, 2 * D], F32))
                xst = st5.enter_context(nc.sbuf_tensor("xst", [128, 2 * D], F32))
                xn2t = st5.enter_context(nc.sbuf_tensor("xn2t", [128, 2 * D], BF16))
                lnst3 = st5.enter_context(nc.sbuf_tensor("lnst3", [128, 64], F32))
                lnbn = st5.enter_context(nc.sbuf_tensor("lnbn", [128, 48], F32))
                ps_t3 = st5.enter_context(nc.psum_tensor("ps_t3", [128, 2048], BF16))

                ev_wo = None
                for dcc in range(DC):
                    ev_wo = dma("scalar", "sWO", wo_sb[:, dcc * D:(dcc + 1) * D],
                                wo_out[dcc * 128:(dcc + 1) * 128, :],
                                deps=[ev_agwo, ev_att_done], chain=False)

                sum8b = lnst3[:, 0:8]
                sq8b = lnst3[:, 8:16]
                m8b = lnst3[:, 16:24]
                negm8b = lnst3[:, 24:32]
                msq8b = lnst3[:, 32:40]
                var8b = lnst3[:, 40:48]
                u8b = lnst3[:, 48:56]
                rstd8b = lnst3[:, 56:64]

                ev_pj = {}
                ev_prjh_ev = {}
                ev_redb = {}
                ev_sqb = {}
                atp_pe = {}
                atp_ev = {}
                ev_r8b_h = {}
                ev_nr8b_h = {}
                ev_x1wr = {}
                ev_xn2 = {}
                tp3_pe = {}
                tp3_ev = {}
                ev_ln2_t = {}
                pst3_prev = {}

                def ln1_smalls(hh):
                    tl = slice(hh * 4, hh * 4 + 4)
                    prog.add("vector",
                             lambda e, tl=tl: e.tensor_scalar_mul(
                                 m8b[:, tl], sum8b[:, tl], 1.0 / D),
                             deps=[ev_redb[4 * hh + 3]], inc=VE)
                    prog.add("vector",
                             lambda e, tl=tl: e.tensor_scalar_mul(
                                 negm8b[:, tl], sum8b[:, tl], -1.0 / D),
                             inc=VE)
                    prog.add("vector",
                             lambda e, tl=tl: e.tensor_tensor(
                                 msq8b[:, tl], m8b[:, tl], m8b[:, tl],
                                 OP.mult),
                             inc=VE)
                    ev_v = prog.add(
                        "vector",
                        lambda e, tl=tl: e.scalar_tensor_tensor(
                            var8b[:, tl], sq8b[:, tl], 1.0 / D, msq8b[:, tl],
                            OP.mult, OP.subtract),
                        deps=[ev_sqb[4 * hh + 3]], inc=VE)
                    prog.add("scalar",
                             lambda e, tl=tl: e.activation(
                                 u8b[:, tl], var8b[:, tl], AF.Ln,
                                 bias=eps_sb[:, :]),
                             deps=[ev_v], inc=SC)
                    ev_r8b_h[hh] = prog.add(
                        "scalar",
                        lambda e, tl=tl: e.activation(
                            rstd8b[:, tl], u8b[:, tl], AF.Exp, scale=-0.5),
                        inc=SC)
                    ev_nr8b_h[hh] = prog.add(
                        "vector",
                        lambda e, tl=tl: e.tensor_tensor(
                            negm8b[:, tl], negm8b[:, tl], rstd8b[:, tl],
                            OP.mult),
                        deps=[ev_r8b_h[hh]], inc=VE)

                def stage3_chain(tt):
                    par = tt % 2
                    ev_xs = dma("sync", "sXS%d" % (tt % 2),
                                xst[:, par * D:(par + 1) * D],
                                xs[tt * 128:(tt + 1) * 128, :],
                                deps=[ev_xn2.get(tt - 2)])
                    ev_ap1 = prog.add(
                        "scalar",
                        lambda e, tt=tt, par=par: e.activation(
                            x1t[:, par * D:(par + 1) * D],
                            prjb[:, tt * D:(tt + 1) * D],
                            AF.Identity,
                            bias=negm8b[:, tt:tt + 1],
                            scale=rstd8b[:, tt:tt + 1]),
                        deps=[ev_nr8b_h[tt // 4], ev_x1wr.get(tt - 2)], inc=SC)
                    ev_x1 = prog.add(
                        "vector",
                        lambda e, par=par: e.tensor_tensor(
                            x1t[:, par * D:(par + 1) * D],
                            x1t[:, par * D:(par + 1) * D],
                            xst[:, par * D:(par + 1) * D], OP.add),
                        deps=[ev_ap1, ev_xs], inc=VE)
                    ev_x1wr[tt] = dma("sync", "sX1W%d" % (tt % 2),
                                      x1_sp[tt * 128:(tt + 1) * 128, :],
                                      x1t[:, par * D:(par + 1) * D],
                                      deps=[ev_x1])

                    sl8 = stats[:, par * 8:par * 8 + 8]
                    mv2 = sl8[:, 0:2]
                    u2s = sl8[:, 2:3]
                    rstd2s = sl8[:, 3:4]
                    nr2s = sl8[:, 4:5]

                    def _bn2(e, par=par):
                        for c in range(4):
                            i = e.bn_stats(
                                lnbn[:, par * 24 + c * 6:par * 24 + (c + 1) * 6],
                                x1t[:, par * D + c * 512:par * D + (c + 1) * 512])
                        return i
                    prog.add("vector", _bn2, deps=[ev_x1], inc=VE)
                    ev_bn2 = prog.add(
                        "vector",
                        lambda e, mv2=mv2, par=par: e.bn_aggr(
                            mv2, lnbn[:, par * 24:par * 24 + 24]),
                        inc=VE)
                    prog.add("scalar",
                             lambda e, u2s=u2s, mv2=mv2: e.activation(
                                 u2s, mv2[:, 1:2], AF.Ln, bias=eps_sb[:, :]),
                             deps=[ev_bn2], inc=SC)
                    ev_rs2 = prog.add(
                        "scalar",
                        lambda e, rstd2s=rstd2s, u2s=u2s: e.activation(
                            rstd2s, u2s, AF.Exp, scale=-0.5),
                        inc=SC)
                    ev_nr2 = prog.add(
                        "vector",
                        lambda e, nr2s=nr2s, mv2=mv2, rstd2s=rstd2s:
                            e.scalar_tensor_tensor(
                                nr2s, mv2[:, 0:1], -1.0, rstd2s,
                                OP.mult, OP.mult),
                        deps=[ev_rs2], inc=VE)
                    ev_ln2 = prog.add(
                        "scalar",
                        lambda e, par=par, nr2s=nr2s, rstd2s=rstd2s:
                            e.activation(
                                xn2t[:, par * D:(par + 1) * D],
                                x1t[:, par * D:(par + 1) * D],
                                AF.Identity, bias=nr2s, scale=rstd2s),
                        deps=[ev_nr2, tp3_pe.get((tt - 2, 1))], inc=SC)
                    ev_ln2_t[tt] = ev_ln2

                def stage3_tp(tt):
                    par = tt % 2
                    ev_ln2 = ev_ln2_t[tt]
                    for half in range(2):
                        def _tp3(e, par=par, half=half):
                            for k in range(8):
                                dcc = half * 8 + k
                                i = e.transpose(
                                    ps_t3[:, (half * 8 + k) * 128:
                                          (half * 8 + k + 1) * 128],
                                    xn2t[:, par * D + dcc * 128:
                                         par * D + (dcc + 1) * 128],
                                    id_sb[:, :])
                            return i
                        pv = (tt, 0) if half else (tt - 1, 1)
                        tp3_pe[(tt, half)] = prog.add(
                            "tensor", _tp3,
                            deps=[ev_ln2, tp3_ev.get(pv),
                                  pst3_prev.get(half)], inc=PE)

                        def _te3(e, tt=tt, half=half):
                            for k in range(8):
                                dcc = half * 8 + k
                                i = e.tensor_copy(
                                    xn2T[:, dcc * TPC + tt * 128:
                                         dcc * TPC + (tt + 1) * 128],
                                    ps_t3[:, (half * 8 + k) * 128:
                                          (half * 8 + k + 1) * 128])
                            return i
                        tp3_ev[(tt, half)] = prog.add(
                            "vector", _te3, deps=[tp3_pe[(tt, half)]], inc=VE)
                        pst3_prev[half] = tp3_ev[(tt, half)]
                    ev_xn2[tt] = tp3_ev[(tt, 1)]

                for tt in range(NT):
                    par = tt % 2
                    ev_ao = None
                    for dcc in range(DC):
                        ev_ao = dma(
                            "sync", "sAOT%d" % (tt % 2),
                            aor[:, (par * DC + dcc) * 128:(par * DC + dcc + 1) * 128],
                            ao_out[dcc % 2][(dcc // 2) * TPC + tt * 128:
                                            (dcc // 2) * TPC + (tt + 1) * 128, :],
                            deps=[ev_a2a_h[dcc % 2], ev_att_done, ev_onrm_last,
                                  atp_pe.get((tt - 2, 1))])
                    for half in range(2):
                        def _atp(e, par=par, half=half):
                            for k in range(8):
                                dcc = half * 8 + k
                                i = e.transpose(
                                    ps_t3[:, (half * 8 + k) * 128:
                                          (half * 8 + k + 1) * 128],
                                    aor[:, (par * DC + dcc) * 128:
                                        (par * DC + dcc + 1) * 128],
                                    id_sb[:, :])
                            return i
                        atp_pe[(tt, half)] = prog.add(
                            "tensor", _atp,
                            deps=[ev_ao, pst3_prev.get(half),
                                  ev_onrm_last, ev_c_all], inc=PE)

                        def _ate(e, par=par, half=half):
                            return e.tensor_copy(
                                aot[:, par * DC * 128 + half * 1024:
                                    par * DC * 128 + (half + 1) * 1024],
                                ps_t3[:, half * 1024:(half + 1) * 1024])
                        atp_ev[(tt, half)] = prog.add(
                            "vector", _ate,
                            deps=[atp_pe[(tt, half)], ev_pj.get(tt - 2)],
                            inc=VE)
                        pst3_prev[half] = atp_ev[(tt, half)]
                    if tt >= 4:
                        stage3_chain(tt - 4)
                    ev_pjh = {}
                    for h2 in range(2):
                        r3 = (2 * tt + h2) % 3

                        def _pjmm(e, par=par, h2=h2, r3=r3):
                            for dcc in range(DC):
                                for oc2 in range(2):
                                    i = e.matmul(
                                        ps6[:, r3 * 1024 + oc2 * 512:
                                            r3 * 1024 + (oc2 + 1) * 512],
                                        aot[:, (par * DC + dcc) * 128:
                                            (par * DC + dcc + 1) * 128],
                                        wo_sb[:, dcc * D + h2 * 1024 + oc2 * 512:
                                              dcc * D + h2 * 1024 + (oc2 + 1) * 512],
                                        start=(dcc == 0), stop=(dcc == DC - 1))
                            return i
                        ev_pjh[h2] = prog.add(
                            "tensor", _pjmm,
                            deps=[atp_ev[(tt, 0)], atp_ev[(tt, 1)], ev_wo,
                                  ev_att_done,
                                  ev_prjh_ev.get(2 * tt + h2 - 3),
                                  ev_onrm_last],
                            inc=PE)
                        ev_prjh_ev[2 * tt + h2] = prog.add(
                            "vector",
                            lambda e, tt=tt, h2=h2, r3=r3: e.tensor_copy(
                                prjb[:, tt * D + h2 * 1024:tt * D + (h2 + 1) * 1024],
                                ps6[:, r3 * 1024:(r3 + 1) * 1024]),
                            deps=[ev_pjh[h2]], inc=VE)
                    ev_pj[tt] = ev_pjh[1]
                    ev_redb[tt] = prog.add(
                        "vector",
                        lambda e, tt=tt: e.reduce_sum(
                            sum8b[:, tt:tt + 1], prjb[:, tt * D:(tt + 1) * D],
                            AX.X),
                        deps=[ev_prjh_ev[2 * tt + 1]], inc=VE)
                    ev_sqb[tt] = prog.add(
                        "scalar",
                        lambda e, tt=tt: e.activation(
                            sqs[:, :], prjb[:, tt * D:(tt + 1) * D], AF.Square,
                            accum_out=sq8b[:, tt:tt + 1]),
                        deps=[ev_prjh_ev[2 * tt + 1]], inc=SC)
                    if tt == 3:
                        ln1_smalls(0)
                    if tt == 7:
                        ln1_smalls(1)
                    if tt >= 4:
                        stage3_tp(tt - 4)
                for tt in range(4, NT):
                    stage3_chain(tt)
                    if tt >= 5:
                        stage3_tp(tt - 1)
                stage3_tp(NT - 1)
                ev_3a_done = tp3_ev[(NT - 1, 1)]
                prog.flush(block, sems)

            with ExitStack() as st6:
                Ebuf = st6.enter_context(nc.sbuf_tensor("Ebuf", [128, NT * D], F32))
                b1bc_sb = st6.enter_context(nc.sbuf_tensor("b1bc_sb", [128, D], F32))
                tailst = st6.enter_context(nc.sbuf_tensor("tailst", [128, 64], F32))
                bnsc = st6.enter_context(nc.sbuf_tensor("bnsc", [128, 96], F32))
                with ExitStack() as st7:
                    Hq = st7.enter_context(
                        nc.sbuf_tensor("Hq", [128, 2 * PCF * TPC], BF16))
                    w0b = st7.enter_context(nc.sbuf_tensor("w0b", [128, 3 * D], BF16))
                    w1b = st7.enter_context(
                        nc.sbuf_tensor("w1b", [128, 2 * PCF * D], BF16))
                    psx1 = st7.enter_context(nc.psum_tensor("psx1", [128, 512], F32))
                    psx2 = st7.enter_context(nc.psum_tensor("psx2", [128, 512], F32))

                    H_ps = [ps6[:, 0:1024], ps6[:, 1024:2048]]
                    E_ps = [ps6[:, 2048:2560], ps6[:, 2560:3072],
                            psx1[:, :], psx2[:, :]]

                    ev_w0 = {}
                    ev_w1 = {}
                    ev_gelu = {}
                    ev_f2 = {}
                    ev_ea = {}
                    ev_tile_final = {}
                    last_eadd = [None]

                    def sched_w0(fc):
                        gate = ev_gelu.get(fc - 3)
                        assert fc < 3 or gate is not None
                        ev_w0[fc] = dma(
                            "sync" if fc < 3 else "gpsimd",
                            "sW0" + "abc"[fc % 3],
                            w0b[:, (fc % 3) * D:(fc % 3 + 1) * D],
                            w0_out[fc * 128:(fc + 1) * 128, :],
                            deps=[ev_agw0, ev_3a_done, ev_x1wr[6],
                                  ev_x1wr[7], gate])

                    def sched_w1(p):
                        for fi in range(PCF):
                            fc = p * PCF + fi
                            gate1 = ev_f2.get((p - 2, 3, 1))
                            assert p < 2 or gate1 is not None
                            ev_w1[fc] = dma(
                                "sync" if p < 2 else "gpsimd",
                                "sW1" + "ab"[p % 2],
                                w1b[:, ((p % 2) * PCF + fi) * D:
                                    ((p % 2) * PCF + fi + 1) * D],
                                w1_out[fc * 128:(fc + 1) * 128, :],
                                deps=[ev_agw1, ev_3a_done, ev_x1wr[6],
                                      ev_x1wr[7], gate1])

                    def ffn1(p, fi):
                        fc = p * PCF + fi
                        hpar = p % 2
                        par = fc % 2

                        def _f1(e, fc=fc, par=par):
                            for dcc in range(DC):
                                for th in range(2):
                                    i = e.matmul(
                                        H_ps[par][:, th * 512:(th + 1) * 512],
                                        w0b[:, (fc % 3) * D + dcc * 128:
                                            (fc % 3) * D + (dcc + 1) * 128],
                                        xn2T[:, dcc * TPC + th * 512:
                                             dcc * TPC + (th + 1) * 512],
                                        start=(dcc == 0), stop=(dcc == DC - 1))
                            return i
                        ev_mm = prog.add(
                            "tensor", _f1,
                            deps=[ev_w0[fc], ev_3a_done, ev_gelu.get(fc - 2)],
                            inc=PE)
                        ev_gelu[fc] = prog.add(
                            "scalar",
                            lambda e, fc=fc, fi=fi, hpar=hpar, par=par:
                                e.activation(
                                    Hq[:, (hpar * PCF + fi) * TPC:
                                       (hpar * PCF + fi + 1) * TPC],
                                    H_ps[par], AF.Gelu,
                                    bias=b0t_sb[:, fc:fc + 1]),
                            deps=[ev_mm, ev_c_all, ev_f2.get((p - 2, 3, 1))],
                            inc=SC)
                        if fc + 3 < NFC:
                            sched_w0(fc + 3)

                    def ffn2_group(p, oc, ttg):
                        hpar = p % 2
                        for t4 in range(4):
                            tt = ttg * 4 + t4

                            def _f2(e, p=p, oc=oc, tt=tt, t4=t4, hpar=hpar):
                                for fi in range(PCF):
                                    i = e.matmul(
                                        E_ps[t4],
                                        Hq[:, (hpar * PCF + fi) * TPC + tt * 128:
                                           (hpar * PCF + fi) * TPC + (tt + 1) * 128],
                                        w1b[:, ((p % 2) * PCF + fi) * D + oc * 512:
                                            ((p % 2) * PCF + fi) * D + (oc + 1) * 512],
                                        start=(fi == 0), stop=(fi == PCF - 1),
                                        skip_group_check=True)
                                return i
                            ev = prog.add(
                                "tensor", _f2,
                                deps=[ev_gelu[p * PCF + PCF - 1],
                                      ev_w1[p * PCF + PCF - 1],
                                      ev_ea.get(("prev", t4))],
                                inc=PE)
                            ev_f2[(p, oc, ttg)] = ev

                            def _eat(e, p=p, oc=oc, tt=tt, t4=t4):
                                dst = Ebuf[:, tt * D + oc * 512:
                                           tt * D + (oc + 1) * 512]
                                if p == 0:
                                    return e.tensor_tensor(
                                        dst, E_ps[t4],
                                        b1bc_sb[:, oc * 512:(oc + 1) * 512],
                                        OP.add)
                                return e.tensor_tensor(dst, E_ps[t4], dst, OP.add)
                            ev_ea[("prev", t4)] = prog.add(
                                "vector", _eat,
                                deps=[ev] + ([ev_b1bc] if p == 0 else []),
                                inc=VE)
                            last_eadd[0] = ev_ea[("prev", t4)]
                            if p == NPC - 1 and oc == 3:
                                ev_tile_final[tt] = ev_ea[("prev", t4)]

                    ev_b1bc = dma("sync", "sC", b1bc_sb[:, :], b1bc[:, :],
                                  deps=[ev_3a_done, ev_x1wr[6], ev_x1wr[7]])

                    for fc in range(3):
                        sched_w0(fc)
                    for fi in range(PCF):
                        ffn1(0, fi)
                    sched_w1(0)
                    sched_w1(1)
                    for p in range(NPC - 1):
                        gi = 0
                        for oc in range(4):
                            for ttg in range(2):
                                ffn2_group(p, oc, ttg)
                                if gi < PCF:
                                    ffn1(p + 1, gi)
                                gi += 1
                        if p + 2 < NPC:
                            sched_w1(p + 2)

                    with ExitStack() as st8:
                        x1r = st8.enter_context(
                            nc.sbuf_tensor("x1r", [128, 2 * D], F32))

                        ev_x1l = {}
                        ev_ywr = {}

                        def x1_load(tt):
                            deps = [ev_gelu[NFC - 1], ev_3a_done,
                                    ev_x1wr[6], ev_x1wr[7]]
                            if tt >= 2:
                                deps.append(ev_ywr[tt - 2])
                            ev_x1l[tt] = dma(
                                "sync", "sX1R%d" % (tt % 2),
                                x1r[:, (tt % 2) * D:(tt % 2 + 1) * D],
                                x1_sp[tt * 128:(tt + 1) * 128, :],
                                deps=deps)

                        ev_tail_rs = {}

                        def tail_stats_group(tg):
                            base = tg * 16
                            u4 = tailst[:, base + 0:base + 4]
                            rstd4 = tailst[:, base + 4:base + 8]
                            nr4 = tailst[:, base + 8:base + 12]
                            mv = tailst[:, 32 + tg * 8:32 + tg * 8 + 8]

                            ev_ag = None
                            for t4 in range(4):
                                tt = tg * 4 + t4

                                def _bn(e, tt=tt, t4=t4):
                                    for c in range(4):
                                        i = e.bn_stats(
                                            bnsc[:, (t4 * 4 + c) * 6:
                                                 (t4 * 4 + c + 1) * 6],
                                            Ebuf[:, tt * D + c * 512:
                                                 tt * D + (c + 1) * 512])
                                    return i
                                prog.add("vector", _bn,
                                         deps=[ev_tile_final[tt]], inc=VE)
                                ev_ag = prog.add(
                                    "vector",
                                    lambda e, t4=t4: e.bn_aggr(
                                        mv[:, t4 * 2:t4 * 2 + 2],
                                        bnsc[:, t4 * 24:(t4 + 1) * 24]),
                                    inc=VE)
                            mean4 = mv[:, 0:8:2]
                            var4 = mv[:, 1:8:2]
                            prog.add("scalar",
                                     lambda e, u4=u4, var4=var4: e.activation(
                                         u4, var4, AF.Ln, bias=eps_sb[:, :]),
                                     deps=[ev_ag], inc=SC)
                            ev_rs = prog.add(
                                "scalar",
                                lambda e, rstd4=rstd4, u4=u4: e.activation(
                                    rstd4, u4, AF.Exp, scale=-0.5),
                                inc=SC)
                            ev_tail_rs[tg] = prog.add(
                                "vector",
                                lambda e, nr4=nr4, mean4=mean4, rstd4=rstd4:
                                    e.scalar_tensor_tensor(
                                        nr4, mean4, -1.0, rstd4,
                                        OP.mult, OP.mult),
                                deps=[ev_rs], inc=VE)

                        def tail_tile_fast(tt):
                            tg, t4 = tt // 4, tt % 4
                            base = tg * 16
                            u1 = tailst[:, base + t4:base + t4 + 1]
                            rstd1 = tailst[:, base + 4 + t4:base + 5 + t4]
                            nr1 = tailst[:, base + 8 + t4:base + 9 + t4]
                            mv = tailst[:, 32 + tg * 8 + t4 * 2:
                                        32 + tg * 8 + t4 * 2 + 2]

                            def _bn(e, tt=tt, t4=t4):
                                for c in range(4):
                                    i = e.bn_stats(
                                        bnsc[:, (t4 * 4 + c) * 6:
                                             (t4 * 4 + c + 1) * 6],
                                        Ebuf[:, tt * D + c * 512:
                                             tt * D + (c + 1) * 512])
                                return i
                            prog.add("vector", _bn,
                                     deps=[ev_tile_final[tt]], inc=VE)
                            ev_ag = prog.add(
                                "vector",
                                lambda e, t4=t4, mv=mv: e.bn_aggr(
                                    mv, bnsc[:, t4 * 24:(t4 + 1) * 24]),
                                inc=VE)
                            prog.add("scalar",
                                     lambda e, u1=u1, mv=mv: e.activation(
                                         u1, mv[:, 1:2], AF.Ln,
                                         bias=eps_sb[:, :]),
                                     deps=[ev_ag], inc=SC)
                            ev_rs = prog.add(
                                "scalar",
                                lambda e, rstd1=rstd1, u1=u1: e.activation(
                                    rstd1, u1, AF.Exp, scale=-0.5),
                                inc=SC)
                            ev_nr = prog.add(
                                "vector",
                                lambda e, nr1=nr1, mv=mv, rstd1=rstd1:
                                    e.scalar_tensor_tensor(
                                        nr1, mv[:, 0:1], -1.0, rstd1,
                                        OP.mult, OP.mult),
                                deps=[ev_rs], inc=VE)
                            Et = Ebuf[:, tt * D:(tt + 1) * D]
                            x1sl = x1r[:, (tt % 2) * D:(tt % 2 + 1) * D]
                            ev_ap = prog.add(
                                "scalar",
                                lambda e, Et=Et, nr1=nr1, rstd1=rstd1:
                                    e.activation(Et, Et, AF.Identity,
                                                 bias=nr1, scale=rstd1),
                                deps=[ev_nr], inc=SC)
                            ev_fin = prog.add(
                                "vector",
                                lambda e, x1sl=x1sl, Et=Et: e.tensor_tensor(
                                    x1sl, Et, x1sl, OP.add),
                                deps=[ev_ap, ev_x1l[tt]], inc=VE)
                            ev_ywr[tt] = dma(
                                "sync", "sY%d" % (tt % 2),
                                y[tt * 128:(tt + 1) * 128, :], x1sl,
                                deps=[ev_fin])

                        def tail_apply(tt):
                            tg, t4 = tt // 4, tt % 4
                            base = tg * 16
                            nr1 = tailst[:, base + 8 + t4:base + 9 + t4]
                            rstd1 = tailst[:, base + 4 + t4:base + 5 + t4]
                            Et = Ebuf[:, tt * D:(tt + 1) * D]
                            x1sl = x1r[:, (tt % 2) * D:(tt % 2 + 1) * D]
                            ev_ap = prog.add(
                                "scalar",
                                lambda e, Et=Et, nr1=nr1, rstd1=rstd1:
                                    e.activation(Et, Et, AF.Identity,
                                                 bias=nr1, scale=rstd1),
                                deps=[ev_tail_rs[tg]], inc=SC)
                            ev_fin = prog.add(
                                "vector",
                                lambda e, x1sl=x1sl, Et=Et: e.tensor_tensor(
                                    x1sl, Et, x1sl, OP.add),
                                deps=[ev_ap, ev_x1l[tt]], inc=VE)
                            ev_ywr[tt] = dma(
                                "sync", "sY%d" % (tt % 2),
                                y[tt * 128:(tt + 1) * 128, :], x1sl,
                                deps=[ev_fin])

                        x1_load(0)
                        x1_load(1)
                        for oc in range(4):
                            ffn2_group(NPC - 1, oc, 0)
                        tail_stats_group(0)
                        for oc in range(4):
                            ffn2_group(NPC - 1, oc, 1)
                            tail_apply(oc)
                            x1_load(oc + 2)
                        for tt in range(4, NT):
                            tail_tile_fast(tt)
                            if tt + 2 < NT:
                                x1_load(tt + 2)

                        prog.add("sync", lambda e: None,
                                 deps=[ev_ywr[NT - 2], ev_ywr[NT - 1]])
                        prog.flush(block, sems)

    return nc


def _stage_inputs(x, wp, wo, W0, b0, W1, b1):
    x = np.ascontiguousarray(np.asarray(x, dtype=np.float32).reshape(S, D))
    wp = np.asarray(wp, dtype=np.float32)
    wo = np.asarray(wo, dtype=np.float32)
    W0 = np.asarray(W0, dtype=np.float32)
    b0 = np.asarray(b0, dtype=np.float32)
    W1 = np.asarray(W1, dtype=np.float32)
    b1 = np.asarray(b1, dtype=np.float32)

    wp5 = wp.reshape(3, 16, 128, DC, 128)
    woT = np.ascontiguousarray(wo.T)
    W0r = W0.reshape(NFC, 128, DC, 128)
    W1T = np.ascontiguousarray(W1.T)
    b0t = np.ascontiguousarray(b0.reshape(NFC, 128).T)
    b1bc = np.ascontiguousarray(np.broadcast_to(b1.reshape(1, D), (128, D)))

    kq = np.arange(128)
    tri = np.where(kq[:, None] <= kq[None, :], 0.0, NEG).astype(np.float32)
    ident = np.eye(128, dtype=ml_dtypes.bfloat16)

    in_maps = []
    for c in range(NCORES):
        blk = wp5[:, 2 * c:2 * c + 2]
        blk = np.transpose(blk, (4, 1, 0, 3, 2))
        wq_im = np.ascontiguousarray(blk.reshape(128, 96 * 128))

        w0s = W0r[8 * c:8 * c + 8]
        w0s = np.ascontiguousarray(
            np.transpose(w0s, (0, 3, 2, 1)).reshape(1024, D))

        in_maps.append({
            "xs": np.ascontiguousarray(x[TPC * c:TPC * (c + 1)]),
            "wqkv": wq_im,
            "wo_sh": np.ascontiguousarray(woT[256 * c:256 * (c + 1)]),
            "w0_sh": w0s,
            "w1_sh": np.ascontiguousarray(W1T[1024 * c:1024 * (c + 1)]),
            "b0t": b0t,
            "b1bc": b1bc,
            "tri": tri,
            "identd": ident,
        })
    return in_maps


def kernel(L0_attn_x, L0_attn_w_project, L0_attn_w_out,
           L0_ffn_W0, L0_ffn_bias0, L0_ffn_W1, L0_ffn_bias1):
    in_maps = _stage_inputs(L0_attn_x, L0_attn_w_project, L0_attn_w_out,
                            L0_ffn_W0, L0_ffn_bias0, L0_ffn_W1, L0_ffn_bias1)
    nc = build_nc()
    res = run_bass_kernel_spmd(nc, in_maps, core_ids=list(range(NCORES)))
    out = np.concatenate([np.asarray(res.results[c]["y"])
                          for c in range(NCORES)], axis=0)
    return out.reshape(1, S, D).astype(np.float32)
